# version 13
# baseline (speedup 1.0000x reference)
import numpy as np
import contextlib
import concourse.bass as bass
import concourse.mybir as mybir
from concourse.bass_utils import run_bass_kernel_spmd
from concourse.alu_op_type import AluOpType as A

F32 = mybir.dt.float32
BF16 = mybir.dt.bfloat16
AF = mybir.ActivationFunctionType
AX = mybir.AxisListType.X

C = 1024
H = 4096
CH = 8
HCH = 32
EPS = 1e-5

S_C1W, S_C2W, S_GNG, S_GNB, S_C2B, S_C1, S_C2, S_GATE, S_N = 0, 12, 24, 28, 32, 33, 34, 35, 40


def build(N=8192, phases=5):
    NS = N // 128
    NT = N // 256
    import os as _os
    nc = bass.Bass("TRN2", target_bir_lowering=False,
                   detect_race_conditions=bool(_os.environ.get("BASS_RACE_CHECK")))

    x_h = nc.dram_tensor("x", [N, C], F32, kind="ExternalInput").ap()
    qkvw_h = nc.dram_tensor("qkv_w", [C, 3 * C], F32, kind="ExternalInput").ap()
    qkvwT_h = nc.dram_tensor("qkv_wT", [3 * C, C], F32, kind="ExternalInput").ap()
    chw1_h = nc.dram_tensor("ch_w1", [C, 256], F32, kind="ExternalInput").ap()
    chb1_h = nc.dram_tensor("ch_b1", [1, 256], F32, kind="ExternalInput").ap()
    chw2_h = nc.dram_tensor("ch_w2", [256, C], F32, kind="ExternalInput").ap()
    chb2_h = nc.dram_tensor("ch_b2", [1, C], F32, kind="ExternalInput").ap()
    projw_h = nc.dram_tensor("proj_w", [C, C], F32, kind="ExternalInput").ap()
    projb_h = nc.dram_tensor("proj_b", [1, C], F32, kind="ExternalInput").ap()
    wfc_h = nc.dram_tensor("w_fc", [H, C], F32, kind="ExternalInput").ap()
    wproj_h = nc.dram_tensor("w_proj", [C, H], F32, kind="ExternalInput").ap()
    uT_h = nc.dram_tensor("uT", [16, H], F32, kind="ExternalInput").ap()
    v_h = nc.dram_tensor("v", [16, C], F32, kind="ExternalInput").ap()
    fbT_h = nc.dram_tensor("fbT", [16, 16], F32, kind="ExternalInput").ap()
    smalls_h = nc.dram_tensor("smalls", [1, S_N], F32, kind="ExternalInput").ap()
    a_hb = nc.dram_tensor("a_scr", [N, C], BF16, kind="Internal").ap()
    x1_hb = nc.dram_tensor("x1_scr", [N, C], F32, kind="Internal").ap()
    f_hb = nc.dram_tensor("f_scr", [N, C], BF16, kind="Internal").ap()
    w1scr = nc.dram_tensor("w1_scr", [H, C], BF16, kind="Internal").ap()
    w2scr = nc.dram_tensor("w2_scr", [C, H], BF16, kind="Internal").ap()
    y_h = nc.dram_tensor("y", [N, C], F32, kind="ExternalOutput").ap()
    dbg_h = nc.dram_tensor("dbg", [N, C], F32, kind="Internal").ap()

    st0 = contextlib.ExitStack()
    G = st0.enter_context
    sem = {k: G(nc.semaphore("sem_" + k))
           for k in ("dX", "dG0", "dG1", "dG2", "dA0", "dA1", "dA2", "dT0", "dT1",
                     "dU0", "dU1", "dV0", "dW0", "dW1", "dZ0", "dZ1", "dZ2", "dB0", "dB1",
                     "dC", "dG3", "dY0", "dY1", "sT", "sV", "sS", "sG")}
    cnt = {k: 0 for k in sem}

    def inc(instr, s, n=1):
        instr.then_inc(sem[s], n)
        cnt[s] += n
        return cnt[s]

    def wge(eng, s, v):
        if barrier_mode[0] and s in ("sS", "sV", "sT"):
            bar(eng)
            return
        if v and v > 0:
            eng.wait_ge(sem[s], v)

    TE, V, S, GP, SY = nc.tensor, nc.vector, nc.scalar, nc.gpsimd, nc.sync

    barrier_mode = [False]

    def bar(eng):
        eng.wait_ge(sem["sS"], cnt["sS"])
        eng.wait_ge(sem["sV"], cnt["sV"])
        eng.wait_ge(sem["sT"], cnt["sT"])

    def wall(eng):
        for k in sem:
            if k.startswith("d") and cnt[k]:
                eng.wait_ge(sem[k], cnt[k])

    def vsy(instr):
        instr.then_inc(sem["sV"], 1)
        cnt["sV"] += 1
        nc.vector.wait_ge(sem["sV"], cnt["sV"])
        return cnt["sV"]

    def ssy(instr):
        instr.then_inc(sem["sS"], 1)
        cnt["sS"] += 1
        nc.scalar.wait_ge(sem["sS"], cnt["sS"])
        return cnt["sS"]

    Wvp = G(nc.sbuf_tensor("Wvp", [128, CH * C], BF16))
    pb_rep = G(nc.sbuf_tensor("pb_rep", [128, C], F32))
    ident_b = G(nc.sbuf_tensor("ident_b", [128, 128], BF16))
    ident_f = G(nc.sbuf_tensor("ident_f", [128, 128], F32))
    ones_row_f = G(nc.sbuf_tensor("ones_row_f", [1, 128], F32))
    ones_col_f = G(nc.sbuf_tensor("ones_col_f", [128, 1], F32))
    ones_col_b = G(nc.sbuf_tensor("ones_col_b", [128, 1], BF16))
    one_one_f = G(nc.sbuf_tensor("one_one_f", [1, 1], F32))
    one_one_b = G(nc.sbuf_tensor("one_one_b", [1, 1], BF16))
    smalls_row = G(nc.sbuf_tensor("smalls_row", [1, S_N], F32))
    smalls = G(nc.sbuf_tensor("smalls_bc", [128, S_N], F32))
    dumpS = G(nc.sbuf_tensor("dumpS", [128, C], BF16))
    V2T = G(nc.sbuf_tensor("V2T", [16, C], BF16))
    uT_g = G(nc.sbuf_tensor("uT_g", [16, H], BF16))
    weights_a = G(nc.sbuf_tensor("weights_a", [128, NS], F32))
    canary = G(nc.sbuf_tensor("canary", [128, 16], F32))

    def scol(i):
        return smalls[:, i:i + 1]

    st1 = contextlib.ExitStack()
    P1 = st1.enter_context
    Wqk = P1(nc.sbuf_tensor("Wqk", [128, CH * 2048], BF16))
    ch1w = P1(nc.sbuf_tensor("ch1w", [128, CH * 256], BF16))
    ch2w = P1(nc.sbuf_tensor("ch2w", [128, 2 * C], BF16))
    chb1_r = P1(nc.sbuf_tensor("chb1_r", [1, 256], F32))
    chb2_r = P1(nc.sbuf_tensor("chb2_r", [1, C], F32))
    stats = P1(nc.sbuf_tensor("stats", [128, 5 * NS], F32))
    m_acc = P1(nc.sbuf_tensor("m_acc", [128, C], F32))
    mid = P1(nc.sbuf_tensor("midpool", [128, 16 * NS], F32))
    shp1 = P1(nc.sbuf_tensor("shp1", [128, 128], F32))
    shm1 = P1(nc.sbuf_tensor("shm1", [128, 128], F32))
    car_m = P1(nc.sbuf_tensor("car_m", [128, 128], F32))
    car_p = P1(nc.sbuf_tensor("car_p", [128, 128], F32))
    iota = P1(nc.sbuf_tensor("iota", [128, 128], F32))
    ones_mat_f = P1(nc.sbuf_tensor("ones_mat_f", [128, 128], F32))
    dumpV = P1(nc.sbuf_tensor("dumpV", [128, C], BF16))


    st_su = contextlib.ExitStack()
    SU = st_su.enter_context
    wtmp = [SU(nc.sbuf_tensor(f"wtmp{i}", [128, 2048], F32)) for i in range(2)]
    uT_f = SU(nc.sbuf_tensor("uT_f", [16, 2048], F32))
    v_f = SU(nc.sbuf_tensor("v_f", [16, C], F32))
    v_b = SU(nc.sbuf_tensor("v_b", [16, C], BF16))
    fbT_f = SU(nc.sbuf_tensor("fbT_f", [16, 16], F32))
    fbT_b = SU(nc.sbuf_tensor("fbT_b", [16, 16], BF16))
    V2b = SU(nc.sbuf_tensor("V2b", [128, 16], BF16))
    pbrow = SU(nc.sbuf_tensor("pbrow", [1, C], F32))
    ps_a = SU(nc.psum_tensor("ps_su_a", [128, 512], F32))
    ps_b = SU(nc.psum_tensor("ps_su_b", [128, 512], F32))
    ps_c = SU(nc.psum_tensor("ps_su_c", [128, 128], F32))
    ps_cb = SU(nc.psum_tensor("ps_su_cb", [16, 128], BF16))

    inc(SY.dma_start(smalls_row[:, :], smalls_h[:, :]), "dX", 16)
    inc(SY.dma_start(uT_f[:, :], uT_h[:, 0:2048]), "dX", 16)
    inc(SY.dma_start(v_f[:, :], v_h[:, :]), "dX", 16)
    inc(SY.dma_start(fbT_f[:, :], fbT_h[:, :]), "dX", 16)
    ev = inc(SY.dma_start(pbrow[:, :], projb_h[:, :]), "dX", 16)
    ev_small = ev

    GP.memset(ones_row_f[:, :], 1.0)
    GP.memset(ones_col_f[:, :], 1.0)
    GP.memset(ones_col_b[:, :], 1.0)
    GP.memset(ones_mat_f[:, :], 1.0)
    GP.memset(one_one_f[:, :], 1.0)
    GP.memset(one_one_b[:, :], 1.0)
    ev_gp0 = inc(GP.iota(iota[:, :], pattern=[[1, 128]], base=0, channel_multiplier=-1,
                         allow_small_or_imprecise_dtypes=True), "sG", 1)
    wge(V, "sG", ev_gp0)
    V.tensor_single_scalar(ident_b[:, :], iota[:, :], 0.0, A.is_equal)
    V.tensor_single_scalar(ident_f[:, :], iota[:, :], 0.0, A.is_equal)
    wge(V, "dX", ev_small)
    V.tensor_copy(v_b[:, :], v_f[:, :])
    ev_v0 = inc(V.tensor_copy(fbT_b[:, :], fbT_f[:, :]), "sV")

    wge(TE, "dX", ev_small)
    wge(TE, "sV", ev_v0)
    TE.matmul(ps_a[:, 0:S_N], ones_row_f[:, :], smalls_row[:, :], start=True, stop=True)
    ev_t = inc(TE.matmul(ps_b[:, 0:512], ones_row_f[:, :], pbrow[:, 0:512], start=True, stop=True), "sT")
    wge(S, "sT", ev_t)
    S.activation(smalls[:, :], ps_a[:, 0:S_N], AF.Copy)
    ev_s = inc(S.activation(pb_rep[:, 0:512], ps_b[:, 0:512], AF.Copy), "sS")
    wge(TE, "sS", ev_s)
    ev_t = inc(TE.matmul(ps_b[:, 0:512], ones_row_f[:, :], pbrow[:, 512:1024], start=True, stop=True), "sT")
    wge(S, "sT", ev_t)
    ev_pb = inc(S.activation(pb_rep[:, 512:1024], ps_b[:, 0:512], AF.Copy), "sS")
    wge(V, "sS", ev_pb)
    ev_h1 = inc(V.tensor_scalar(uT_g[:, 0:2048], uT_f[:, :], smalls[0:16, S_GATE:S_GATE + 1], None, A.mult), "sV")
    wge(SY, "sV", ev_h1)
    ev_l2 = inc(SY.dma_start(uT_f[:, :], uT_h[:, 2048:4096]), "dX", 16)
    inc(SY.dma_start(chb1_r[:, :], chb1_h[:, :]), "dX", 16)
    inc(SY.dma_start(chb2_r[:, :], chb2_h[:, :]), "dX", 16)
    wge(V, "dX", cnt["dX"])
    ev_ug = inc(V.tensor_scalar(uT_g[:, 2048:4096], uT_f[:, :], smalls[0:16, S_GATE:S_GATE + 1], None, A.mult), "sV")
    for ci in range(CH):
        wge(TE, "sV", ev_v0 if ci == 0 else ev_v2e)
        ev_t = inc(TE.matmul(ps_c[:, 0:16], v_b[:, ci * 128:(ci + 1) * 128], fbT_b[:, :],
                             start=True, stop=True), "sT")
        wge(S, "sT", ev_t)
        ev_s = inc(S.activation(V2b[:, :], ps_c[:, 0:16], AF.Copy), "sS")
        wge(TE, "sS", ev_s)
        ev_t = inc(TE.transpose(ps_cb[:, :], V2b[:, :], ident_b[:, :]), "sT")
        wge(S, "sT", ev_t)
        ev_v2e = inc(S.activation(V2T[0:16, ci * 128:(ci + 1) * 128], ps_cb[:, :], AF.Copy), "sS")

    wge(V, "sG", ev_gp0)
    V.tensor_single_scalar(shp1[:, :], iota[:, :], 1.0, A.is_equal)
    V.tensor_single_scalar(shm1[:, :], iota[:, :], -1.0, A.is_equal)
    V.tensor_single_scalar(car_m[:, :], iota[:, :], -127.0, A.is_equal)
    V.tensor_single_scalar(car_p[:, :], iota[:, :], 127.0, A.is_equal)

    ev_ld = {}
    ev_cast = {}
    loads = [("qk", ci) for ci in range(CH)] + [("c1", ci) for ci in range(CH)] + [("c2", ci) for ci in range(2)]
    for idx, (kind, ci) in enumerate(loads):
        b = idx % 2
        if idx >= 2:
            wge(SY, "sV", ev_cast[idx - 2])
        if kind == "qk":
            src, w = qkvw_h[ci * 128:(ci + 1) * 128, 0:2048], 2048
        elif kind == "c1":
            src, w = chw1_h[ci * 128:(ci + 1) * 128, :], 256
        else:
            src, w = chw2_h[ci * 128:(ci + 1) * 128, :], C
        ev_ld[idx] = inc(SY.dma_start(wtmp[b][:, 0:w], src), f"dW{b}", 16)
        wge(V, f"dW{b}", ev_ld[idx])
        if kind == "qk":
            e = V.tensor_copy(Wqk[:, ci * 2048:(ci + 1) * 2048], wtmp[b][:, 0:2048])
        elif kind == "c1":
            e = V.tensor_copy(ch1w[:, ci * 256:(ci + 1) * 256], wtmp[b][:, 0:256])
        else:
            e = V.tensor_copy(ch2w[:, ci * C:(ci + 1) * C], wtmp[b][:, 0:C])
        ev_cast[idx] = inc(e, "sV")
    ev_setup_v = cnt["sV"]
    st_su.close()
    if phases <= 1:
        wall(SY)
        st1.close()
        st0.close()
        return nc

    x0_sb = [P1(nc.sbuf_tensor(f"x0_{i}", [128, 1024], F32)) for i in range(3)]
    a_sb = [P1(nc.sbuf_tensor(f"a_{i}", [128, 1024], BF16)) for i in range(3)]
    aT_sb = [P1(nc.sbuf_tensor(f"aT_{i}", [128, 8 * 256], BF16, is_transpose_dest=True)) for i in range(2)]
    qk_sb = [P1(nc.sbuf_tensor(f"qk_{i}", [128, 2048], BF16)) for i in range(2)]
    p1c = [P1(nc.sbuf_tensor(f"p1c_{i}", [128, 12], F32)) for i in range(2)]
    st_p1p = contextlib.ExitStack()
    qp = [st_p1p.enter_context(nc.psum_tensor(f"qp{j}", [128, 1024], F32)) for j in range(3)]

    ev_x0, ev_aw, ev_at, ev_apply, ev_qkv, ev_qke, ev_vstat, ev_ssq = {}, {}, {}, {}, {}, {}, {}, {}
    ev_tpe2 = {}
    ev_tp, ev_tpe, ev_psfree = {}, {}, {}
    ev_ln1, ev_ma = {}, {}
    V.memset(m_acc[:, :], 0.0)

    def p1_load(t):
        if t >= NT:
            return
        for j in range(2):
            u = 2 * t + j
            if u >= 3:
                wge(GP, "sV", max(ev_apply[u - 3], ev_ln1.get(u - 3, 0)))
            ev_x0[u] = inc(GP.dma_start(x0_sb[u % 3][:, :], x_h[u * 128:(u + 1) * 128, :]),
                           f"dA{u % 3}", 16)

    def p1_ln(t):
        if t >= NT:
            return
        for j in range(2):
            u = 2 * t + j
            x0v = x0_sb[u % 3][:, :]
            pc = p1c[u % 2]
            wge(V, f"dA{u % 3}", ev_x0[u])
            vsy(V.reduce_sum(pc[:, 0:1], x0v, axis=AX))
            ev_ln1[u] = vsy(V.scalar_tensor_tensor(dumpV[:, :], x0v, 1.0, x0v, A.mult, A.mult,
                                                   accum_out=pc[:, 1:2]))
            vsy(V.tensor_scalar(pc[:, 2:3], pc[:, 0:1], 1.0 / C, None, A.mult))
            vsy(V.tensor_scalar(pc[:, 3:4], pc[:, 1:2], 1.0 / C, EPS, A.mult, A.add))
            vsy(V.tensor_tensor(pc[:, 4:5], pc[:, 2:3], pc[:, 2:3], A.mult))
            e1 = vsy(V.tensor_tensor(pc[:, 5:6], pc[:, 3:4], pc[:, 4:5], A.subtract))
            wge(S, "sV", e1)
            e2 = inc(S.activation(pc[:, 6:7], pc[:, 5:6], AF.Sqrt), "sS")
            wge(V, "sS", e2)
            vsy(V.reciprocal(pc[:, 7:8], pc[:, 6:7]))
            e3 = vsy(V.scalar_tensor_tensor(pc[:, 8:9], pc[:, 2:3], -1.0, pc[:, 7:8], A.mult, A.mult))
            wge(S, "sV", e3)
            if u >= 3:
                wge(V, f"dG{u % 3}", ev_aw.get(u - 3, 0))
            ev_apply[u] = inc(V.tensor_scalar(a_sb[u % 3][:, :], x0v, pc[:, 7:8], pc[:, 8:9],
                                              A.mult, A.add), "sV")
            ev_ma[u] = inc(V.tensor_tensor(m_acc[:, :], m_acc[:, :], a_sb[u % 3][:, :], A.add), "sV")
            wge(GP, "sV", ev_apply[u])
            ev_aw[u] = inc(GP.dma_start(a_hb[u * 128:(u + 1) * 128, :], a_sb[u % 3][:, :]),
                           f"dG{u % 3}", 16)
            wge(SY, f"dG{u % 3}", ev_aw[u])
            if u >= 4:
                wge(SY, "sT", ev_qkv[(t - 2, j)])
            for ci in range(4):
                e = inc(SY.dma_start(aT_sb[t % 2][:, ci * 256 + j * 128: ci * 256 + (j + 1) * 128],
                                     a_hb[u * 128:(u + 1) * 128, ci * 128:(ci + 1) * 128],
                                     transpose=True), f"dT{t % 2}", 16)
            ev_tpe[u] = e
            wge(S, f"dG{u % 3}", ev_aw[u])
            if u >= 4:
                wge(S, "sT", ev_qkv[(t - 2, j)])
            for ci in range(4, CH):
                e = inc(S.dma_start(aT_sb[t % 2][:, ci * 256 + j * 128: ci * 256 + (j + 1) * 128],
                                    a_hb[u * 128:(u + 1) * 128, ci * 128:(ci + 1) * 128],
                                    transpose=True), f"dU{t % 2}", 16)
            ev_tpe2[u] = e

    wge(TE, "sS", cnt["sS"])
    wge(TE, "sV", cnt["sV"])
    wge(SY, "sS", cnt["sS"])
    wge(SY, "sV", cnt["sV"])
    wge(SY, "sT", cnt["sT"])
    wge(GP, "sS", cnt["sS"])
    wge(GP, "sV", cnt["sV"])
    p1_load(0)
    p1_ln(0)
    p1_load(1)
    for t in range(NT):
        p1_ln(t + 1)
        p1_load(t + 2)
        for j in range(2):
            u = 2 * t + j
            qi, ki = (2 * u) % 3, (2 * u + 1) % 3
            wge(TE, f"dT{t % 2}", ev_tpe[2 * t + 1])
            wge(TE, f"dU{t % 2}", ev_tpe2[2 * t + 1])
            wge(TE, "sV", ev_setup_v)
            wge(TE, "sS", ev_psfree.get(qi, 0))
            wge(TE, "sS", ev_psfree.get(ki, 0))
            for ci in range(CH):
                for n in range(4):
                    pj = qp[qi] if n < 2 else qp[ki]
                    e = TE.matmul(pj[:, (n % 2) * 512:(n % 2 + 1) * 512],
                                  aT_sb[t % 2][:, ci * 256 + j * 128: ci * 256 + (j + 1) * 128],
                                  Wqk[:, ci * 2048 + n * 512: ci * 2048 + (n + 1) * 512],
                                  start=(ci == 0), stop=(ci == CH - 1))
            ev_qkv[(t, j)] = inc(e, "sT")
            wge(S, "sT", ev_qkv[(t, j)])
            if u >= 2:
                wge(S, "sV", ev_vstat[u - 2])
            S.activation(qk_sb[u % 2][:, 0:1024], qp[qi][:, :], AF.Copy,
                         accum_out=stats[:, 0 * NS + u:0 * NS + u + 1])
            ev_qke[u] = inc(S.activation(qk_sb[u % 2][:, 1024:2048], qp[ki][:, :],
                                         AF.Copy, accum_out=stats[:, 1 * NS + u:1 * NS + u + 1]), "sS")
            S.activation(dumpS[:, :], qp[qi][:, :], AF.Square,
                         accum_out=stats[:, 2 * NS + u:2 * NS + u + 1])
            ev_ssq[u] = inc(S.activation(dumpS[:, :], qp[ki][:, :], AF.Square,
                                         accum_out=stats[:, 3 * NS + u:3 * NS + u + 1]), "sS")
            ev_psfree[qi] = ev_ssq[u]
            ev_psfree[ki] = ev_ssq[u]
            wge(V, "sS", ev_qke[u])
            ev_vstat[u] = inc(V.scalar_tensor_tensor(
                dumpV[:, :], qk_sb[u % 2][:, 0:1024], 1.0, qk_sb[u % 2][:, 1024:2048],
                A.mult, A.mult, accum_out=stats[:, 4 * NS + u:4 * NS + u + 1]), "sV")

    ev_p1_s = cnt["sS"]
    ev_p1_v = cnt["sV"]
    st_p1p.close()
    if phases == 21:
        wge(SY, "sS", cnt["sS"])
        wge(SY, "sV", cnt["sV"])
        inc(SY.dma_start(y_h[128:256, 0:5 * NS], stats[:, :]), "dX", 16)
        inc(SY.dma_start(y_h[0:128, 0:C], m_acc[:, :]), "dX", 16)
        SY.wait_ge(sem["dX"], cnt["dX"])
        st1.close()
        st0.close()
        return nc
    if phases == 22:
        wge(SY, "sS", cnt["sS"])
        wge(SY, "sV", cnt["sV"])
        wge(SY, "sT", cnt["sT"])
        for u in range(NS):
            e = inc(SY.dma_start(a_sb[0][:, :], a_hb[u * 128:(u + 1) * 128, :]), "dX", 16)
            wge(S, "dX", e)
            es = inc(S.activation(x0_sb[0][:, :], a_sb[0][:, :], AF.Copy), "sS")
            wge(SY, "sS", es)
            e2 = inc(SY.dma_start(y_h[u * 128:(u + 1) * 128, :], x0_sb[0][:, :]), "dX", 16)
            wge(SY, "dX", e2)
        SY.wait_ge(sem["dX"], cnt["dX"])
        st1.close()
        st0.close()
        return nc
    if phases <= 2:
        wall(SY)
        st1.close()
        st0.close()
        return nc

    st_mp = contextlib.ExitStack()
    MP = st_mp.enter_context
    ps_m = MP(nc.psum_tensor("ps_m", [128, 16], F32))
    ps_r1 = MP(nc.psum_tensor("ps_r1", [1, 512], F32))
    ps_r2 = MP(nc.psum_tensor("ps_r2", [1, 512], F32))
    ps_bc = MP(nc.psum_tensor("ps_bc", [128, 128], F32))
    ps_sh = MP(nc.psum_tensor("ps_sh", [128, NS], F32))
    ps_sh2 = MP(nc.psum_tensor("ps_sh2", [128, NS], F32))
    m_accb = MP(nc.sbuf_tensor("m_accb", [128, C], BF16))
    m_cols = MP(nc.sbuf_tensor("m_cols", [128, CH], BF16))
    ch_cols = MP(nc.sbuf_tensor("ch_cols", [128, CH], F32))
    gc = MP(nc.sbuf_tensor("gcols", [128, 48], F32))
    grow = MP(nc.sbuf_tensor("grow", [1, 128], F32))
    h1rowb = MP(nc.sbuf_tensor("h1rowb", [1, 256], BF16))
    h1colb = MP(nc.sbuf_tensor("h1colb", [128, 2], BF16))
    chrow = MP(nc.sbuf_tensor("chrow", [1, C], F32))

    def m_arr(i):
        return mid[:, i * NS:(i + 1) * NS]

    barrier_mode[0] = False
    if phases == 31:
        wge(SY, "sS", cnt["sS"])
        wge(SY, "sV", cnt["sV"])
        inc(SY.dma_start(y_h[384:512, 0:5 * NS], stats[:, :]), "dX", 16)
    V.memset(gc[:, :], 0.0)
    V.memset(grow[:, :], 0.0)
    wge(V, "sV", ev_p1_v)
    ev = inc(V.tensor_copy(m_accb[:, :], m_acc[:, :]), "sV")
    wge(TE, "sV", ev)
    for ci in range(CH):
        e = TE.matmul(ps_m[:, ci:ci + 1], m_accb[:, ci * 128:(ci + 1) * 128], ones_col_b[:, :],
                      start=(ci == 0), stop=(ci == CH - 1))
    ev = inc(e, "sT")
    wge(S, "sT", ev)
    ev = inc(S.activation(m_cols[:, 0:CH], ps_m[:, 0:CH], AF.Copy, scale=1.0 / N), "sS")
    wge(TE, "sS", ev)
    for ci in range(CH):
        e = TE.matmul(ps_r1[:, 0:256], m_cols[:, ci:ci + 1], ch1w[:, ci * 256:(ci + 1) * 256],
                      start=(ci == 0), stop=(ci == CH - 1))
    ev = inc(e, "sT")
    wge(V, "sT", ev)
    vsy(V.tensor_tensor(grow[:, 0:128], ps_r1[0:1, 0:128], chb1_r[:, 0:128], A.add))
    vsy(V.tensor_tensor(chrow[:, 0:128], ps_r1[0:1, 128:256], chb1_r[:, 128:256], A.add))
    vsy(V.tensor_scalar_max(h1rowb[:, 0:128], grow[:, 0:128], 0.0))
    ev = vsy(V.tensor_scalar_max(h1rowb[:, 128:256], chrow[:, 0:128], 0.0))
    wge(TE, "sV", ev)
    for s_ in range(2):
        e = TE.matmul(ps_m[:, 8 + s_:9 + s_], h1rowb[0:1, s_ * 128:(s_ + 1) * 128], one_one_b[:, :],
                      start=(s_ == 0), stop=(s_ == 1))
    ev = inc(e, "sT")
    wge(S, "sT", ev)
    ev = inc(S.activation(h1colb[:, :], ps_m[:, 8:10], AF.Copy), "sS")
    wge(TE, "sS", ev)
    for s_ in range(2):
        for n in range(2):
            psx = ps_r1 if n == 0 else ps_r2
            e = TE.matmul(psx[:, 0:512], h1colb[:, s_:s_ + 1],
                          ch2w[:, s_ * C + n * 512: s_ * C + (n + 1) * 512],
                          start=(s_ == 0), stop=(s_ == 1))
    ev = inc(e, "sT")
    wge(V, "sT", ev)
    vsy(V.tensor_tensor(chrow[:, 0:512], ps_r1[0:1, 0:512], chb2_r[:, 0:512], A.add))
    ev_chpre = vsy(V.tensor_tensor(chrow[:, 512:1024], ps_r2[0:1, 0:512], chb2_r[:, 512:1024], A.add))

    sqv, skv, sqqv, skkv, sqkv = (stats[:, i * NS:(i + 1) * NS] for i in range(5))
    muq, muk, sgqk, sgq2, sgk2, num, den, base = (m_arr(i) for i in range(8))
    wge(V, "sS", ev_p1_s)
    vsy(V.tensor_scalar(muq, sqv, 1.0 / C, None, A.mult))
    vsy(V.tensor_scalar(muk, skv, 1.0 / C, None, A.mult))
    vsy(V.tensor_tensor(sgqk, sqv, skv, A.mult))
    vsy(V.scalar_tensor_tensor(sgqk, sgqk, -1.0 / C, sqkv, A.mult, A.add))
    vsy(V.tensor_scalar(sgqk, sgqk, 1.0 / (C - 1), None, A.mult))
    vsy(V.tensor_tensor(sgq2, sqv, sqv, A.mult))
    vsy(V.scalar_tensor_tensor(sgq2, sgq2, -1.0 / C, sqqv, A.mult, A.add))
    vsy(V.tensor_scalar(sgq2, sgq2, 1.0 / (C - 1), None, A.mult))
    vsy(V.tensor_tensor(sgk2, skv, skv, A.mult))
    vsy(V.scalar_tensor_tensor(sgk2, sgk2, -1.0 / C, skkv, A.mult, A.add))
    vsy(V.tensor_scalar(sgk2, sgk2, 1.0 / (C - 1), None, A.mult))
    vsy(V.tensor_tensor(num, muq, muk, A.mult))
    vsy(V.tensor_scalar(num, num, 2.0, scol(S_C1), A.mult, A.add))
    vsy(V.tensor_scalar(sgqk, sgqk, 2.0, scol(S_C2), A.mult, A.add))
    vsy(V.tensor_tensor(num, num, sgqk, A.mult))
    vsy(V.tensor_tensor(den, muq, muq, A.mult))
    vsy(V.tensor_tensor(muk, muk, muk, A.mult))
    vsy(V.tensor_tensor(den, den, muk, A.add))
    vsy(V.tensor_scalar(den, den, scol(S_C1), None, A.add))
    vsy(V.tensor_tensor(sgq2, sgq2, sgk2, A.add))
    vsy(V.tensor_scalar(sgq2, sgq2, scol(S_C2), None, A.add))
    vsy(V.tensor_tensor(den, den, sgq2, A.mult))
    vsy(V.tensor_scalar(den, den, 1e-7, None, A.add))
    vsy(V.reciprocal(den, den))
    vsy(V.tensor_tensor(num, num, den, A.mult))
    ev_base = vsy(V.tensor_tensor(base, num, num, A.mult))

    am1, ap1, hc0, hc1, hc2, hc3, acc, A1 = (m_arr(i) for i in range(8, 16))
    hcs = (hc0, hc1, hc2, hc3)

    def shift(dst, src, evw, dirn, psum):
        wge(TE, "sV", evw)
        if dirn == "m1":
            TE.matmul(psum[:, :], shp1[:, :], src, start=True, stop=False)
            e = TE.matmul(psum[:, 1:NS], car_m[:, :], src[:, 0:NS - 1], start=False, stop=True)
        else:
            TE.matmul(psum[:, :], shm1[:, :], src, start=True, stop=False)
            e = TE.matmul(psum[:, 0:NS - 1], car_p[:, :], src[:, 1:NS], start=False, stop=True)
        e = inc(e, "sT")
        wge(S, "sT", e)
        return inc(S.activation(dst, psum[:, :], AF.Copy), "sS")

    e1 = shift(am1, base, ev_base, "m1", ps_sh)
    e2 = shift(ap1, base, ev_base, "p1", ps_sh2)
    wge(V, "sS", e2)
    for c in range(4):
        vsy(V.tensor_scalar(hcs[c], base, scol(S_C1W + c * 3 + 1), None, A.mult))
        vsy(V.scalar_tensor_tensor(hcs[c], am1, scol(S_C1W + c * 3 + 0), hcs[c], A.mult, A.add))
        vsy(V.scalar_tensor_tensor(hcs[c], ap1, scol(S_C1W + c * 3 + 2), hcs[c], A.mult, A.add))
    for c in range(4):
        ev = vsy(V.reduce_sum(gc[:, c:c + 1], hcs[c], axis=AX))
    wge(S, "sV", ev)
    for c in range(4):
        e = S.activation(dumpS[:, 0:NS], hcs[c], AF.Square, accum_out=gc[:, 4 + c:5 + c])
    ev = inc(e, "sS")
    wge(V, "sS", ev)
    vsy(V.tensor_tensor(gc[:, 8:9], gc[:, 0:1], gc[:, 1:2], A.add))
    vsy(V.tensor_tensor(gc[:, 9:10], gc[:, 2:3], gc[:, 3:4], A.add))
    vsy(V.tensor_tensor(gc[:, 10:11], gc[:, 4:5], gc[:, 5:6], A.add))
    ev = vsy(V.tensor_tensor(gc[:, 11:12], gc[:, 6:7], gc[:, 7:8], A.add))
    wge(TE, "sV", ev)
    ev = inc(TE.matmul(ps_bc[:, 0:4], ones_mat_f[:, :], gc[:, 8:12], start=True, stop=True), "sT")
    wge(S, "sT", ev)
    ev = inc(S.activation(gc[:, 20:24], ps_bc[:, 0:4], AF.Copy), "sS")
    wge(V, "sS", ev)
    vsy(V.tensor_scalar(gc[:, 24:26], gc[:, 20:22], 1.0 / (2 * N), None, A.mult))
    vsy(V.tensor_scalar(gc[:, 26:28], gc[:, 22:24], 1.0 / (2 * N), EPS, A.mult, A.add))
    vsy(V.tensor_tensor(gc[:, 28:30], gc[:, 24:26], gc[:, 24:26], A.mult))
    ev = vsy(V.tensor_tensor(gc[:, 26:28], gc[:, 26:28], gc[:, 28:30], A.subtract))
    wge(S, "sV", ev)
    ev = inc(S.activation(gc[:, 30:32], gc[:, 26:28], AF.Sqrt), "sS")
    wge(V, "sS", ev)
    vsy(V.reciprocal(gc[:, 26:27], gc[:, 30:31]))
    ev = vsy(V.reciprocal(gc[:, 27:28], gc[:, 31:32]))
    for c in range(4):
        g = c // 2
        vsy(V.tensor_scalar(gc[:, 32 + c:33 + c], gc[:, 26 + g:27 + g], scol(S_GNG + c), None, A.mult))
        vsy(V.tensor_tensor(gc[:, 36 + c:37 + c], gc[:, 24 + g:25 + g], gc[:, 32 + c:33 + c], A.mult))
        ev = vsy(V.scalar_tensor_tensor(gc[:, 36 + c:37 + c], gc[:, 36 + c:37 + c], -1.0,
                                        scol(S_GNB + c), A.mult, A.add))
    wge(S, "sV", ev)
    for c in range(4):
        e = S.activation(hcs[c], hcs[c], AF.Relu, bias=gc[:, 36 + c:37 + c], scale=gc[:, 32 + c:33 + c])
    ev_gn = inc(e, "sS")
    wge(V, "sS", ev_gn)
    evv = vsy(V.memset(acc[:, :], 0.0))
    for c in range(4):
        e1 = shift(am1, hcs[c], evv, "m1", ps_sh)
        e2 = shift(ap1, hcs[c], 0, "p1", ps_sh2)
        wge(V, "sS", e2)
        vsy(V.scalar_tensor_tensor(acc, am1, scol(S_C2W + c * 3 + 0), acc, A.mult, A.add))
        vsy(V.scalar_tensor_tensor(acc, hcs[c], scol(S_C2W + c * 3 + 1), acc, A.mult, A.add))
        evv = vsy(V.scalar_tensor_tensor(acc, ap1, scol(S_C2W + c * 3 + 2), acc, A.mult, A.add))
    vsy(V.tensor_scalar(acc, acc, scol(S_C2B), None, A.add))
    ev_a1 = vsy(V.tensor_tensor(A1, base, acc, A.add))
    wge(S, "sV", max(ev_a1, ev_chpre))
    AF_SIG = AF.Copy if phases == 35 else AF.Sigmoid
    S.activation(chrow[:, :], chrow[:, :], AF_SIG)
    ev_sig = ssy(S.activation(A1, A1, AF_SIG))
    wge(TE, "sS", ev_sig)
    for ci in range(CH):
        e = TE.matmul(ps_m[:, ci:ci + 1], chrow[0:1, ci * 128:(ci + 1) * 128], one_one_f[:, :],
                      start=(ci == 0), stop=(ci == CH - 1))
    ev = inc(e, "sT")
    wge(S, "sT", ev)
    ev_chc = inc(S.activation(ch_cols[:, :], ps_m[:, 0:CH], AF.Copy), "sS")
    ev_exp = inc(S.activation(A1, A1, AF.Copy if phases == 35 else AF.Exp, accum_out=gc[:, 3:4]), "sS")
    wge(TE, "sS", ev_exp)
    ev = inc(TE.matmul(ps_bc[:, 4:5], ones_mat_f[:, :], gc[:, 3:4], start=True, stop=True), "sT")
    wge(S, "sT", ev)
    ev = inc(S.activation(gc[:, 4:5], ps_bc[:, 4:5], AF.Copy), "sS")
    wge(V, "sS", ev)
    ev = vsy(V.reciprocal(gc[:, 5:6], gc[:, 4:5]))
    ev_wts = vsy(V.tensor_scalar(weights_a[:, :], A1, gc[:, 5:6], None, A.mult))

    wvt_all = MP(nc.sbuf_tensor("wvt_all", [128, CH * C], BF16))
    ppb = MP(nc.sbuf_tensor("ppb", [128, CH * C], BF16))
    pwf = [MP(nc.sbuf_tensor(f"pwf_{i}", [128, C], F32)) for i in range(2)]
    ps_vp = MP(nc.psum_tensor("ps_vp", [128, C], F32))
    ev_prep = {}
    wge(SY, "sS", cnt["sS"])
    wge(SY, "sV", cnt["sV"])
    wge(SY, "sT", cnt["sT"])
    for cv in range(CH):
        b = cv % 2
        if cv >= 2:
            wge(SY, "sV", ev_prep[cv - 2])
        ev_l1 = inc(SY.dma_start(pwf[b][:, :], projw_h[cv * 128:(cv + 1) * 128, :]), f"dW{b}", 16)
        wge(V, f"dW{b}", ev_l1)
        wge(V, "sS", ev_chc)
        ev_p = inc(V.tensor_scalar(ppb[:, cv * C:(cv + 1) * C], pwf[b][:, :],
                                   ch_cols[:, cv:cv + 1], None, A.mult), "sV")
        wge(SY, "sV", ev_p)
        ev_l2 = inc(SY.dma_start(pwf[b][:, :], qkvwT_h[2048 + cv * 128:2048 + (cv + 1) * 128, :]),
                    f"dW{b}", 16)
        wge(V, f"dW{b}", ev_l2)
        ev_prep[cv] = inc(V.tensor_copy(wvt_all[:, cv * C:(cv + 1) * C], pwf[b][:, :]), "sV")
    ev_vpe = {}
    for mi in range(CH):
        if mi >= 1:
            wge(TE, "sS", ev_vpe[mi - 1])
        for cv in range(CH):
            wge(TE, "sV", ev_prep[cv])
            for n in range(2):
                e = TE.matmul(ps_vp[:, n * 512:(n + 1) * 512],
                              wvt_all[:, cv * C + mi * 128: cv * C + (mi + 1) * 128],
                              ppb[:, cv * C + n * 512: cv * C + (n + 1) * 512],
                              start=(cv == 0), stop=(cv == CH - 1))
        ev = inc(e, "sT")
        wge(S, "sT", ev)
        ev_vpe[mi] = inc(S.activation(Wvp[:, mi * C:(mi + 1) * C], ps_vp[:, :], AF.Copy), "sS")
    if phases in (33, 34):
        wge(SY, "sV", cnt["sV"])
        inc(SY.dma_start(y_h[256:384, 16:16 + NS], weights_a[:, :]), "dX", 16)
    barrier_mode[0] = False
    ev_mid_done_s = cnt["sS"]
    ev_mid_done_v = cnt["sV"]
    if phases in (31, 33, 35):
        wge(SY, "sV", cnt["sV"])
        wge(SY, "sS", cnt["sS"])
        wge(SY, "sT", cnt["sT"])
        inc(SY.dma_start(y_h[0:128, 0:NS], weights_a[:, :]), "dX", 16)
        inc(SY.dma_start(y_h[0:128, 256:256 + CH], ch_cols[:, :]), "dX", 16)
        inc(SY.dma_start(y_h[0:128, 288:288 + NS], mid[:, 15 * NS:16 * NS]), "dX", 16)
        inc(SY.dma_start(y_h[0:128, 296:296 + NS], mid[:, 7 * NS:8 * NS]), "dX", 16)
        inc(SY.dma_start(y_h[0:128, 304:304 + NS], mid[:, 10 * NS:11 * NS]), "dX", 16)
        inc(SY.dma_start(y_h[0:128, 308:308 + NS], mid[:, 0 * NS:1 * NS]), "dX", 16)
        inc(SY.dma_start(y_h[0:128, 312:312 + NS], mid[:, 5 * NS:6 * NS]), "dX", 16)
        inc(SY.dma_start(y_h[0:128, 316:316 + NS], mid[:, 6 * NS:7 * NS]), "dX", 16)
        inc(SY.dma_start(y_h[0:128, 320:320 + NS], mid[:, 2 * NS:3 * NS]), "dX", 16)
        inc(SY.dma_start(y_h[0:128, 324:324 + 8], smalls[:, 32:40]), "dX", 16)
        inc(SY.dma_start(y_h[0:128, 384:384 + 48], gc[:, :]), "dX", 16)
        inc(SY.dma_start(y_h[0:128, 640:640 + 5 * NS], stats[:, :]), "dX", 16)
        EXIT31 = True
        with nc.sbuf_tensor("dbg_f", [128, 1024], F32) as tmpd, \
             nc.sbuf_tensor("dbg_t", [128, 256], BF16, is_transpose_dest=True) as tmpt:
            ev = inc(S.activation(tmpd[:, :], Wvp[:, 0:1024], AF.Copy), "sS")
            wge(SY, "sS", ev)
            inc(SY.dma_start(y_h[128:256, :], tmpd[:, :]), "dX", 16)
            ev = inc(SY.dma_start(tmpt[:, :], a_hb[0:256, 0:128], transpose=True), "dX", 16)
            wge(S, "dX", ev)
            ev = inc(S.activation(tmpd[:, 0:256], tmpt[:, :], AF.Copy), "sS")
            wge(SY, "sS", ev)
            inc(SY.dma_start(y_h[256:384, 0:256], tmpd[:, 0:256]), "dX", 16)
            SY.wait_ge(sem["dX"], cnt["dX"])
        st_mp.close()
        st1.close()
        st0.close()
        return nc
    st_mp.close()
    st1.close()
    if phases <= 3:
        wall(SY)
        st0.close()
        return nc

    st2 = contextlib.ExitStack()
    P2 = st2.enter_context
    W1p = P2(nc.sbuf_tensor("W1p", [128, CH * H], BF16))
    W2eT = P2(nc.sbuf_tensor("W2eT", [128, HCH * C], BF16, is_transpose_dest=True))
    st_2as = contextlib.ExitStack()
    P2A = st_2as.enter_context
    atp = [P2A(nc.sbuf_tensor(f"atp_{i}", [128, 8 * 256], BF16, is_transpose_dest=True)) for i in range(2)]
    x0p = [P2A(nc.sbuf_tensor(f"x0p_{i}", [128, 1024], F32)) for i in range(3)]
    zw = [P2A(nc.sbuf_tensor(f"zw_{i}", [128, 1024], F32)) for i in range(1)]
    cols2a = P2A(nc.sbuf_tensor("cols2a", [128, 64], F32))
    p2c = [cols2a[:, 0:12], cols2a[:, 12:24]]
    wfc_f = [P2A(nc.sbuf_tensor(f"wfc_{i}", [128, C], F32)) for i in range(1)]
    wps_f = wfc_f
    W1e_g = P2A(nc.sbuf_tensor("W1e_g", [128, 2 * C], BF16))
    w1eT = [P2A(nc.sbuf_tensor(f"w1eT_{i}", [128, 256], BF16, is_transpose_dest=True)) for i in range(2)]
    W2e_sl = P2A(nc.sbuf_tensor("W2e_sl", [128, C], BF16))
    hilo = P2A(nc.sbuf_tensor("hilo", [128, 512], F32))
    hi_s = hilo[:, 0:256]
    lo_s = hilo[:, 256:512]
    bcol = cols2a[:, 24:40]
    wpsc = cols2a[:, 40:64]
    st_2ap = contextlib.ExitStack()
    z_ps = [st_2ap.enter_context(nc.psum_tensor(f"z_ps{j}", [128, C], F32)) for j in range(2)]
    bld_ps = [st_2ap.enter_context(nc.psum_tensor(f"bld_ps{j}", [128, 512], F32)) for j in range(2)]


    ev2_x0, ev2_at, ev2_z, ev2_zwe, ev2_x1, ev2_x1w = {}, {}, {}, {}, {}, {}
    ev2_tpe2 = {}
    ev2_a, ev2_tp, ev2_tpe = {}, {}, {}
    ev_w1e, ev_w1pe, ev_w1w = {}, {}, {}
    ev_wpsc, ev_w2sl, ev_w2te, ev_w2w = {}, {}, {}, {}
    bl_ev = {}
    bl_ctr = [0]

    bld_bufs = [wfc_f[0], x0p[2]]

    def bld_load(src_ap):
        b = bl_ctr[0] % 2
        if bl_ctr[0] >= 2:
            s_, v_ = bl_ev[bl_ctr[0] - 2]
            wge(SY, s_, v_)
        e = inc(SY.dma_start(bld_bufs[b][:, :], src_ap), f"dB{b}", 16)
        bl_ctr[0] += 1
        return b, e

    def bld_done(sem_name, val):
        bl_ev[bl_ctr[0] - 1] = (sem_name, val)

    def p2a_load(t):
        if t >= NT:
            return
        for j in range(2):
            u = 2 * t + j
            if u >= 3:
                wge(GP, f"dG{u % 3}", ev2_x1w[u - 3])
            ev2_x0[u] = inc(GP.dma_start(x0p[u % 3][:, :], x_h[u * 128:(u + 1) * 128, :]),
                            f"dA{u % 3}", 16)
        if t >= 2:
            wge(SY, "sT", ev2_z[(t - 2, 1)])
            wge(S, "sT", ev2_z[(t - 2, 1)])
        for j in range(2):
            u = 2 * t + j
            for ci in range(4):
                e = inc(SY.dma_start(atp[t % 2][:, ci * 256 + j * 128: ci * 256 + (j + 1) * 128],
                                     a_hb[u * 128:(u + 1) * 128, ci * 128:(ci + 1) * 128],
                                     transpose=True), f"dT{t % 2}", 16)
            ev2_tpe[u] = e
            for ci in range(4, CH):
                e = inc(S.dma_start(atp[t % 2][:, ci * 256 + j * 128: ci * 256 + (j + 1) * 128],
                                    a_hb[u * 128:(u + 1) * 128, ci * 128:(ci + 1) * 128],
                                    transpose=True), f"dU{t % 2}", 16)
            ev2_tpe2[u] = e

    def p2a_tp(t):
        pass


    def bld_wfc(kk):
        b, e = bld_load(wfc_h[kk * 128:(kk + 1) * 128, :])
        wge(S, f"dB{b}", e)
        ev = inc(S.activation(dumpS[:, :], bld_bufs[b][:, :], AF.Abs,
                              accum_out=bcol[:, b * 8:b * 8 + 1]), "sS")
        wge(V, "sS", ev)
        vsy(V.tensor_scalar(bcol[:, b * 8 + 1:b * 8 + 2], bcol[:, b * 8:b * 8 + 1], 1.0 / C, 1e-5, A.mult, A.max))
        vsy(V.tensor_scalar(bcol[:, b * 8 + 2:b * 8 + 3], bcol[:, b * 8 + 1:b * 8 + 2], 0.5, None, A.mult))
        vsy(V.tensor_scalar(bcol[:, b * 8 + 3:b * 8 + 4], bcol[:, b * 8 + 1:b * 8 + 2], -0.5, None, A.mult))
        if kk >= 2:
            wge(V, f"dZ{kk % 2}", ev_w1w.get(kk - 2, 0))
        for hf in range(4):
            sl_ = slice(hf * 256, (hf + 1) * 256)
            V.tensor_scalar(hi_s[:, :], bld_bufs[b][:, sl_], bcol[:, b * 8 + 2:b * 8 + 3],
                            bcol[:, b * 8 + 1:b * 8 + 2], A.is_gt, A.mult)
            V.tensor_scalar(lo_s[:, :], bld_bufs[b][:, sl_], bcol[:, b * 8 + 3:b * 8 + 4],
                            bcol[:, b * 8 + 1:b * 8 + 2], A.is_lt, A.mult)
            e = V.tensor_tensor(W1e_g[:, (kk % 2) * C + hf * 256:(kk % 2) * C + (hf + 1) * 256],
                                hi_s[:, :], lo_s[:, :], A.subtract)
        ev_w1e[kk] = inc(e, "sV")
        bld_done("sV", ev_w1e[kk])
        wge(SY, "sV", ev_w1e[kk])
        ew = inc(SY.dma_start(w1scr[kk * 128:(kk + 1) * 128, :], W1e_g[:, (kk % 2) * C:(kk % 2 + 1) * C]),
                 f"dZ{kk % 2}", 16)
        ev_w1w[kk] = ew
        wge(SY, f"dZ{kk % 2}", ew)
        if kk % 2 == 1:
            g = kk // 2
            for ci in range(CH):
                pp = bld_ps[ci % 2]
                if g >= 1 or ci >= 2:
                    pk = (g, ci - 2) if ci >= 2 else (g - 1, CH - 2 + ci)
                    wge(SY, "sV", ev_w1pe[pk])
                et = inc(SY.dma_start(w1eT[ci % 2][:, :],
                                      w1scr[g * 256:(g + 1) * 256, ci * 128:(ci + 1) * 128],
                                      transpose=True), f"dT{ci % 2}", 16)
                wge(TE, "sV", ev_ug)
                if g >= 1 or ci >= 2:
                    wge(TE, "sV", ev_w1pe[pk])
                e = TE.matmul(pp[:, 0:256], V2T[:, ci * 128:(ci + 1) * 128],
                              uT_g[:, g * 256:(g + 1) * 256], start=True, stop=True)
                ev = inc(e, "sT")
                wge(V, "sT", ev)
                wge(V, f"dT{ci % 2}", et)
                ev_w1pe[(g, ci)] = inc(V.tensor_tensor(W1p[:, ci * H + g * 256: ci * H + (g + 1) * 256],
                                                       bld_ps[ci % 2][:, 0:256], w1eT[ci % 2][:, :],
                                                       A.add), "sV")

    def bld_wpscale(k):
        for sl in range(4):
            b, e = bld_load(wproj_h[k * 128:(k + 1) * 128, sl * C:(sl + 1) * C])
            wge(S, f"dB{b}", e)
            ev_wpsc[(k, sl)] = inc(S.activation(dumpS[:, :], bld_bufs[b][:, :], AF.Abs,
                                                accum_out=bcol[:, 4 + sl:5 + sl]), "sS")
            bld_done("sS", ev_wpsc[(k, sl)])
        wge(V, "sS", ev_wpsc[(k, 3)])
        vsy(V.tensor_tensor(bcol[:, 4:5], bcol[:, 4:5], bcol[:, 5:6], A.add))
        vsy(V.tensor_tensor(bcol[:, 6:7], bcol[:, 6:7], bcol[:, 7:8], A.add))
        vsy(V.tensor_tensor(bcol[:, 4:5], bcol[:, 4:5], bcol[:, 6:7], A.add))
        vsy(V.tensor_scalar(wpsc[:, 3 * k:3 * k + 1], bcol[:, 4:5], 1.0 / H, 1e-5, A.mult, A.max))
        vsy(V.tensor_scalar(wpsc[:, 3 * k + 1:3 * k + 2], wpsc[:, 3 * k:3 * k + 1], 0.5, None, A.mult))
        ev = vsy(V.tensor_scalar(wpsc[:, 3 * k + 2:3 * k + 3], wpsc[:, 3 * k:3 * k + 1],
                                 -0.5, None, A.mult))
        ev_wpsc[(k, "done")] = ev

    def bld_wptern(si):
        k, sl = si // 4, si % 4
        b, e = bld_load(wproj_h[k * 128:(k + 1) * 128, sl * C:(sl + 1) * C])
        wge(V, f"dB{b}", e)
        wge(V, "sV", ev_wpsc[(k, "done")])
        if si >= 1:
            wge(V, "dC", ev_w2te.get((si - 1, 1), 0))
        for hf in range(4):
            sl_ = slice(hf * 256, (hf + 1) * 256)
            V.tensor_scalar(hi_s[:, :], bld_bufs[b][:, sl_], wpsc[:, 3 * k + 1:3 * k + 2],
                            wpsc[:, 3 * k:3 * k + 1], A.is_gt, A.mult)
            V.tensor_scalar(lo_s[:, :], bld_bufs[b][:, sl_], wpsc[:, 3 * k + 2:3 * k + 3],
                            wpsc[:, 3 * k:3 * k + 1], A.is_lt, A.mult)
            e = V.tensor_tensor(W2e_sl[:, hf * 256:(hf + 1) * 256],
                                hi_s[:, :], lo_s[:, :], A.subtract)
        ev_w2sl[si] = inc(e, "sV")
        bld_done("sV", ev_w2sl[si])
        wge(SY, "sV", ev_w2sl[si])
        ew = inc(SY.dma_start(w2scr[k * 128:(k + 1) * 128, sl * 1024:(sl + 1) * 1024], W2e_sl[:, :]),
                 "dZ2", 16)
        ev_w2w[si] = ew
        wge(SY, "dZ2", ew)
        for m in range(8):
            hm = sl * 8 + m
            e = inc(SY.dma_start(W2eT[:, hm * C + k * 128: hm * C + (k + 1) * 128],
                                 w2scr[k * 128:(k + 1) * 128, hm * 128:(hm + 1) * 128],
                                 transpose=True), "dC", 16)
        ev_w2te[(si, 0)] = ev_w2te[(si, 1)] = e

    def bld_units(t):
        if t < HCH:
            bld_wfc(t)
        if t < CH:
            bld_wpscale(t)
        if 8 <= t < 24:
            bld_wptern((t - 8) * 2)
            bld_wptern((t - 8) * 2 + 1)

    for s_ in ("dG0", "dG1", "dG2"):
        wge(SY, s_, cnt[s_])
        wge(S, s_, cnt[s_])
    wge(SY, "sS", ev_mid_done_s)
    wge(SY, "sV", ev_mid_done_v)
    wge(SY, "sT", cnt["sT"])
    wge(TE, "sS", ev_mid_done_s)
    wge(TE, "sV", ev_mid_done_v)
    barrier_mode[0] = False
    for _k in range(CH):
        bld_wpscale(_k)
    for _si in range(4 * CH):
        bld_wptern(_si)
    for _kk in range(HCH):
        bld_wfc(_kk)
    barrier_mode[0] = False
    ev_bld_s = cnt["sS"]
    ev_bld_v = cnt["sV"]
    wge(TE, "sV", ev_bld_v)
    wge(TE, "sS", ev_bld_s)
    wge(SY, "sV", ev_bld_v)
    wge(GP, "sV", ev_bld_v)
    wge(GP, "sS", ev_bld_s)
    wge(GP, "sT", cnt["sT"])
    p2a_load(0)
    for t in range(NT):
        p2a_tp(t)
        for j in range(2):
            u = 2 * t + j
            wge(TE, f"dT{t % 2}", ev2_tpe[2 * t + 1])
            wge(TE, f"dU{t % 2}", ev2_tpe2[2 * t + 1])
            wge(TE, "sS", ev_mid_done_s)
            if t >= 1:
                wge(TE, "sS", ev2_zwe[2 * (t - 1) + j])
            for ci in range(CH):
                for n in range(2):
                    e = TE.matmul(z_ps[j][:, n * 512:(n + 1) * 512],
                                  atp[t % 2][:, ci * 256 + j * 128: ci * 256 + (j + 1) * 128],
                                  Wvp[:, ci * C + n * 512: ci * C + (n + 1) * 512],
                                  start=(ci == 0), stop=(ci == CH - 1))
            ev2_z[(t, j)] = inc(e, "sT")
            zv = zw[0][:, :]
            pc = p2c[u % 2]
            wge(S, "sT", ev2_z[(t, j)])
            wge(S, "sV", ev_mid_done_v)
            if u >= 1:
                wge(S, "sV", ev2_x1.get(u - 1, 0))
            ev2_zwe[u] = inc(S.activation(zv, z_ps[j][:, :], AF.Copy,
                                          scale=(1.0 if phases == 34 else weights_a[:, u:u + 1])), "sS")
            if phases in (34,) and u < 2:
                wge(SY, "sS", ev2_zwe[u])
                inc(SY.dma_start(y_h[u * 128:(u + 1) * 128, :], zv), "dX", 16)
            wge(V, "sS", ev2_zwe[u])
            ev_pbd = vsy(V.scalar_tensor_tensor(zv, zv, 1.0, pb_rep[:, :], A.mult, A.add,
                                                accum_out=pc[:, 0:1]))
            if phases == 33 and 2 <= u < 4:
                wge(SY, "sV", ev_pbd)
                inc(SY.dma_start(y_h[u * 128:(u + 1) * 128, :], zv), "dX", 16)
            ev = vsy(V.tensor_scalar(pc[:, 2:3], pc[:, 0:1], 1.0 / C, None, A.mult))
            wge(S, "sV", ev)
            e2 = inc(S.activation(dumpS[:, :], zv, AF.Square, accum_out=pc[:, 1:2]), "sS")
            wge(V, "sS", e2)
            vsy(V.tensor_scalar(pc[:, 3:4], pc[:, 1:2], 1.0 / C, EPS, A.mult, A.add))
            vsy(V.tensor_tensor(pc[:, 4:5], pc[:, 2:3], pc[:, 2:3], A.mult))
            e1 = vsy(V.tensor_tensor(pc[:, 5:6], pc[:, 3:4], pc[:, 4:5], A.subtract))
            wge(S, "sV", e1)
            e2 = inc(S.activation(pc[:, 6:7], pc[:, 5:6], AF.Sqrt), "sS")
            wge(V, "sS", e2)
            vsy(V.reciprocal(pc[:, 7:8], pc[:, 6:7]))
            e3 = vsy(V.scalar_tensor_tensor(pc[:, 8:9], pc[:, 2:3], -1.0, pc[:, 7:8], A.mult, A.mult))
            wge(S, "sV", e3)
            V.tensor_scalar(zv, zv, pc[:, 7:8], pc[:, 8:9], A.mult, A.add)
            wge(V, f"dA{u % 3}", ev2_x0[u])
            ev2_x1[u] = inc(V.tensor_tensor(x0p[u % 3][:, :], zv, x0p[u % 3][:, :], A.add), "sV")
            wge(GP, "sV", ev2_x1[u])
            ev2_x1w[u] = inc(GP.dma_start(x1_hb[u * 128:(u + 1) * 128, :], x0p[u % 3][:, :]),
                             f"dG{u % 3}", 16)

        p2a_load(t + 1)

    ev_2a_s = cnt["sS"]
    ev_2a_v = cnt["sV"]
    ev_2a_t = cnt["sT"]
    st_2ap.close()
    if phases == 32:
        for u in range(min(NS, 4)):
            wall(SY)
            ev = inc(SY.dma_start(x0p[0][:, :], x1_hb[u * 128:(u + 1) * 128, :]), "dX", 16)
            wge(SY, "dX", ev)
            inc(SY.dma_start(y_h[u * 128:(u + 1) * 128, :], x0p[0][:, :]), "dX", 16)
        SY.wait_ge(sem["dX"], cnt["dX"])
        st_2as.close()
        st2.close()
        st0.close()
        return nc
    if phases in (33, 34):
        wge(SY, "sS", cnt["sS"])
        wge(SY, "sV", cnt["sV"])
        inc(SY.dma_start(y_h[256:384, 0:NS], weights_a[:, :]), "dX", 16)
        ew = inc(S.activation(zw[0][:, :], Wvp[:, 0:1024], AF.Copy), "sS")
        wge(SY, "sS", ew)
        inc(SY.dma_start(y_h[384:512, :], zw[0][:, :]), "dX", 16)
    st_2as.close()
    if phases <= 4 or phases in (33, 34):
        wall(SY)
        SY.wait_ge(sem["dX"], cnt["dX"])
        st2.close()
        st0.close()
        return nc

    x1p = [P2(nc.sbuf_tensor(f"x1p_{i}", [128, 2048], F32)) for i in range(2)]
    f_sb = [P2(nc.sbuf_tensor(f"f_{i}", [128, 2048], BF16)) for i in range(2)]
    ftp = [P2(nc.sbuf_tensor(f"ftp_{i}", [128, 8 * 256], BF16, is_transpose_dest=True)) for i in range(2)]
    r_s = [P2(nc.sbuf_tensor(f"r_s{i}", [128, 512], F32)) for i in range(2)]
    r2T = [P2(nc.sbuf_tensor(f"r2T_{i}", [128, 512], BF16)) for i in range(2)]
    p3c = [P2(nc.sbuf_tensor(f"p3c_{i}", [128, 12], F32)) for i in range(2)]
    dumpV2 = P2(nc.sbuf_tensor("dumpV2", [128, C], BF16))
    o2e = P2(nc.sbuf_tensor("o2e", [128, 1024], F32))
    st_2bp = contextlib.ExitStack()
    h_ps = [st_2bp.enter_context(nc.psum_tensor(f"h_ps{j}", [128, 512], F32)) for j in range(2)]
    o2_ps = [st_2bp.enter_context(nc.psum_tensor(f"o2_ps{j}", [128, C], F32)) for j in range(3)]

    ev3_x1, ev3_fw, ev3_ft, ev3_fap = {}, {}, {}, {}
    ev3_tpe2, ev3_ft2, ev3_o2ev = {}, {}, {}
    ev3_tp, ev3_tpe = {}, {}
    ev3_h, ev3_r, ev3_r2, ev3_o2, ev3_x2, ev3_x2w = {}, {}, {}, {}, {}, {}

    def p2b_load(t):
        if t >= NT:
            return
        if t >= 2:
            wge(GP, f"dY{t % 2}", ev3_x2w[t - 2])
            wge(GP, "sV", ev3_x2[(t - 2, 1)])
        xv = x1_hb[t * 256:(t + 1) * 256, :]
        inc(GP.dma_start(x1p[t % 2][:, 0:1024], xv[0:128, :]), f"dA{t % 2}", 16)
        ev3_x1[t] = inc(GP.dma_start(x1p[t % 2][:, 1024:2048], xv[128:256, :]), f"dA{t % 2}", 16)

    def p2b_ln(t):
        if t >= NT:
            return
        for j in range(2):
            u = 2 * t + j
            xv = x1p[t % 2][:, j * 1024:(j + 1) * 1024]
            pc = p3c[u % 2]
            wge(V, f"dA{t % 2}", ev3_x1[t])
            vsy(V.reduce_sum(pc[:, 0:1], xv, axis=AX))
            vsy(V.scalar_tensor_tensor(dumpV2[:, :], xv, 1.0, xv, A.mult, A.mult,
                                       accum_out=pc[:, 1:2]))
            vsy(V.tensor_scalar(pc[:, 2:3], pc[:, 0:1], 1.0 / C, None, A.mult))
            vsy(V.tensor_scalar(pc[:, 3:4], pc[:, 1:2], 1.0 / C, EPS, A.mult, A.add))
            vsy(V.tensor_tensor(pc[:, 4:5], pc[:, 2:3], pc[:, 2:3], A.mult))
            e1 = vsy(V.tensor_tensor(pc[:, 5:6], pc[:, 3:4], pc[:, 4:5], A.subtract))
            wge(S, "sV", e1)
            e2 = inc(S.activation(pc[:, 6:7], pc[:, 5:6], AF.Sqrt), "sS")
            wge(V, "sS", e2)
            vsy(V.reciprocal(pc[:, 7:8], pc[:, 6:7]))
            e3 = vsy(V.scalar_tensor_tensor(pc[:, 8:9], pc[:, 2:3], -1.0, pc[:, 7:8], A.mult, A.mult))
            if t >= 2:
                wge(V, f"dG{(2 * (t - 2) + 1) % 4}", ev3_fw.get(2 * (t - 2) + 1, 0))
            ev3_fap[u] = inc(V.tensor_scalar(f_sb[t % 2][:, j * 1024:(j + 1) * 1024], xv,
                                             pc[:, 7:8], pc[:, 8:9], A.mult, A.add), "sV")
        if phases == 40:
            ev3_fw[t] = cnt[f"dG{t % 4}"]
            ev3_ft[t] = cnt["sS"]
            return
        for j in range(2):
            u = 2 * t + j
            wge(GP, "sV", ev3_fap[u])
            ev3_fw[u] = inc(GP.dma_start(f_hb[u * 128:(u + 1) * 128, :],
                                         f_sb[t % 2][:, j * 1024:(j + 1) * 1024]), f"dG{u % 4}", 16)
            wge(SY, f"dG{u % 4}", ev3_fw[u])
            if t >= 2 and j == 0:
                wge(SY, "sT", ev3_h.get((t - 2, 15), 0))
            for ci in range(CH):
                e = inc(SY.dma_start(ftp[t % 2][:, ci * 256 + j * 128: ci * 256 + (j + 1) * 128],
                                     f_hb[u * 128:(u + 1) * 128, ci * 128:(ci + 1) * 128],
                                     transpose=True), f"dT{t % 2}", 16)
            ev3_tpe[u] = e
            ev3_tpe2[u] = cnt["dU0"] if t % 2 == 0 else cnt["dU1"]
        ev3_ft[t] = ev3_tpe[2 * t + 1]
        ev3_ft2[t] = ev3_tpe2[2 * t + 1]

    wge(TE, "sS", ev_2a_s)
    wge(TE, "sV", ev_2a_v)
    wge(TE, "dC", cnt["dC"])
    wge(SY, "sS", ev_2a_s)
    wge(SY, "sV", ev_2a_v)
    wge(SY, "sT", cnt["sT"])
    wge(S, "sT", cnt["sT"])
    wge(GP, "sS", ev_2a_s)
    wge(GP, "sV", ev_2a_v)
    wge(GP, "sT", cnt["sT"])
    for s_ in ("dG0", "dG1", "dG2"):
        wge(GP, s_, cnt[s_])
    p2b_load(0)
    p2b_ln(0)
    p2b_load(1)
    for t in range(NT):
        p2b_ln(t + 1)
        if phases in (40, 41):
            for m in range(2):
                u = 2 * t + m
                wge(V, f"dA{t % 2}", ev3_x1[t])
                wge(V, "sV", ev3_fap[u])
                ev3_x2[(t, m)] = inc(V.tensor_scalar(x1p[t % 2][:, m * 1024:(m + 1) * 1024],
                                                     x1p[t % 2][:, m * 1024:(m + 1) * 1024],
                                                     1.0, None, A.mult), "sV")
            wge(SY, "sV", ev3_x2[(t, 1)])
            yv = y_h[t * 256:(t + 1) * 256, :]
            inc(SY.dma_start(yv[0:128, :], x1p[t % 2][:, 0:1024]), f"dY{t % 2}", 16)
            ev3_x2w[t] = inc(SY.dma_start(yv[128:256, :], x1p[t % 2][:, 1024:2048]), f"dY{t % 2}", 16)
            p2b_load(t + 2)
            continue
        nchunk = 0
        for g in range(17):
            if g < 16:
                wge(TE, f"dT{t % 2}", ev3_ft[t])
                wge(TE, f"dU{t % 2}", ev3_ft2[t])
                wge(TE, "sS", ev_2a_s)
                if g >= 2 or t >= 1:
                    wge(TE, "sV", ev3_r2.get((t, g - 2), ev3_r2.get((t - 1, 14 + g), 0)))
                for hh in range(2):
                    kk = 2 * g + hh
                    for ci in range(CH):
                        e = TE.matmul(h_ps[g % 2][:, hh * 256:(hh + 1) * 256],
                                      W1p[:, ci * H + kk * 128: ci * H + (kk + 1) * 128],
                                      ftp[t % 2][:, ci * 256:(ci + 1) * 256],
                                      start=(hh == 0 and ci == 0), stop=(hh == 1 and ci == CH - 1))
                ev3_h[(t, g)] = inc(e, "sT")
                wge(S, "sT", ev3_h[(t, g)])
                if g >= 2:
                    wge(S, "sV", ev3_r2[(t, g - 2)])
                ev3_r[(t, g)] = inc(S.activation(r_s[g % 2][:, :], h_ps[g % 2][:, :], AF.Relu), "sS")
                wge(V, "sS", ev3_r[(t, g)])
                ev3_r2[(t, g)] = inc(V.tensor_tensor(r2T[g % 2][:, :], r_s[g % 2][:, :],
                                                     r_s[g % 2][:, :], A.mult), "sV")
            if g >= 1 and phases != 42:
                gg = g - 1
                wge(TE, "sV", ev3_r2[(t, gg)])
                if gg == 0 and t >= 1:
                    wge(TE, "sS", ev3_o2ev[t - 1])
                if gg == 0 and t >= 2:
                    wge(TE, "sV", ev3_x2[(t - 2, 1)])
                for hh in range(2):
                    kk = 2 * gg + hh
                    for m in range(2):
                        for n in range(2):
                            e = TE.matmul(o2_ps[m * (1 + t % 2)][:, n * 512:(n + 1) * 512],
                                          r2T[gg % 2][:, hh * 256 + m * 128: hh * 256 + (m + 1) * 128],
                                          W2eT[:, kk * C + n * 512: kk * C + (n + 1) * 512],
                                          start=(kk == 0), stop=(kk == HCH - 1))
                    nchunk += 1
        if phases == 42:
            ev3_o2[t] = inc(TE.matmul(o2_ps[0][:, 0:256], W1p[:, 0:128], ftp[t % 2][:, 0:256],
                                      start=True, stop=True), "sT")
            for m in range(2):
                u = 2 * t + m
                wge(V, "sV", ev3_r2[(t, 15)])
                wge(V, f"dA{t % 2}", ev3_x1[t])
                ev3_x2[(t, m)] = inc(V.tensor_scalar(x1p[t % 2][:, m * 1024:(m + 1) * 1024],
                                                     x1p[t % 2][:, m * 1024:(m + 1) * 1024],
                                                     1.0, None, A.mult), "sV")
            wge(SY, "sV", ev3_x2[(t, 1)])
            yv = y_h[t * 256:(t + 1) * 256, :]
            inc(SY.dma_start(yv[0:128, :], x1p[t % 2][:, 0:1024]), f"dY{t % 2}", 16)
            ev3_x2w[t] = inc(SY.dma_start(yv[128:256, :], x1p[t % 2][:, 1024:2048]), f"dY{t % 2}", 16)
            p2b_load(t + 2)
            continue
        ev3_o2[t] = inc(e, "sT")
        if phases == 44:
            for m in range(2):
                wge(V, "sT", ev3_o2[t])
                wge(V, f"dA{t % 2}", ev3_x1[t])
                ev3_x2[(t, m)] = inc(V.tensor_scalar(x1p[t % 2][:, m * 1024:(m + 1) * 1024],
                                                     x1p[t % 2][:, m * 1024:(m + 1) * 1024],
                                                     1.0, None, A.mult), "sV")
            wge(SY, "sV", ev3_x2[(t, 1)])
            yv = y_h[t * 256:(t + 1) * 256, :]
            inc(SY.dma_start(yv[0:128, :], x1p[t % 2][:, 0:1024]), f"dY{t % 2}", 16)
            ev3_x2w[t] = inc(SY.dma_start(yv[128:256, :], x1p[t % 2][:, 1024:2048]), f"dY{t % 2}", 16)
            p2b_load(t + 2)
            continue
        for m in range(2):
            u = 2 * t + m
            pc = p3c[u % 2]
            psrc = o2_ps[m * (1 + t % 2)][:, :]
            o2v = o2e[:, :] if m == 0 else psrc
            wge(S, "sT", ev3_o2[t])
            if m == 0 and t >= 1:
                wge(S, "sV", ev3_x2[(t - 1, 0)])
            eo = inc(S.activation(o2e[:, :] if m == 0 else dumpS[:, :], psrc, AF.Copy,
                                  accum_out=pc[:, 0:1]), "sS")
            e2 = inc(S.activation(dumpS[:, :], psrc, AF.Square, accum_out=pc[:, 1:2]), "sS")
            if m == 0:
                ev3_o2ev[t] = e2
            wge(V, "sS", e2)
            vsy(V.tensor_scalar(pc[:, 2:3], pc[:, 0:1], 1.0 / C, None, A.mult))
            vsy(V.tensor_scalar(pc[:, 3:4], pc[:, 1:2], 1.0 / C, EPS, A.mult, A.add))
            vsy(V.tensor_tensor(pc[:, 4:5], pc[:, 2:3], pc[:, 2:3], A.mult))
            e1 = vsy(V.tensor_tensor(pc[:, 5:6], pc[:, 3:4], pc[:, 4:5], A.subtract))
            wge(S, "sV", e1)
            e2 = inc(S.activation(pc[:, 6:7], pc[:, 5:6], AF.Sqrt), "sS")
            wge(V, "sS", e2)
            vsy(V.reciprocal(pc[:, 7:8], pc[:, 6:7]))
            vsy(V.scalar_tensor_tensor(pc[:, 8:9], pc[:, 2:3], -1.0, pc[:, 7:8], A.mult, A.mult))
            xv = x1p[t % 2][:, m * 1024:(m + 1) * 1024]
            V.tensor_scalar(xv, xv, pc[:, 8:9], None, A.add)
            ev3_x2[(t, m)] = inc(V.scalar_tensor_tensor(xv, o2v, pc[:, 7:8], xv,
                                                        A.mult, A.add), "sV")
        wge(GP, "sV", ev3_x2[(t, 1)])
        yv = y_h[t * 256:(t + 1) * 256, :]
        inc(GP.dma_start(yv[0:128, :], x1p[t % 2][:, 0:1024]), f"dY{t % 2}", 16)
        ev3_x2w[t] = inc(GP.dma_start(yv[128:256, :], x1p[t % 2][:, 1024:2048]), f"dY{t % 2}", 16)
        p2b_load(t + 2)

    wall(SY)
    st_2bp.close()
    st2.close()
    st0.close()
    return nc


_NC_CACHE = {}


def kernel(L0_attn_x, L0_attn_qkv_w, L0_attn_c1, L0_attn_c2, L0_attn_conv1_w,
           L0_attn_gn_g, L0_attn_gn_b, L0_attn_conv2_w, L0_attn_conv2_b,
           L0_attn_ch_w1, L0_attn_ch_b1, L0_attn_ch_w2, L0_attn_ch_b2,
           L0_attn_proj_w, L0_attn_proj_b, L0_ffn_fast_b, L0_ffn_w_fc,
           L0_ffn_w_proj, L0_ffn_u, L0_ffn_v, L0_ffn_gate):
    x = np.asarray(L0_attn_x, dtype=np.float32)
    B, N, Cx = x.shape
    f32 = lambda a: np.ascontiguousarray(np.asarray(a, dtype=np.float32))
    qkv_w = f32(L0_attn_qkv_w)
    smalls = np.zeros((1, S_N), dtype=np.float32)
    smalls[0, S_C1W:S_C1W + 12] = f32(L0_attn_conv1_w).reshape(-1)
    smalls[0, S_C2W:S_C2W + 12] = f32(L0_attn_conv2_w).reshape(-1)
    smalls[0, S_GNG:S_GNG + 4] = f32(L0_attn_gn_g).reshape(-1)
    smalls[0, S_GNB:S_GNB + 4] = f32(L0_attn_gn_b).reshape(-1)
    smalls[0, S_C2B] = np.float32(np.asarray(L0_attn_conv2_b).reshape(-1)[0])
    smalls[0, S_C1] = np.float32(np.asarray(L0_attn_c1))
    smalls[0, S_C2] = np.float32(np.asarray(L0_attn_c2))
    smalls[0, S_GATE] = np.float32(np.asarray(L0_ffn_gate))

    common = {
        "qkv_w": qkv_w,
        "qkv_wT": np.ascontiguousarray(qkv_w.T),
        "ch_w1": f32(L0_attn_ch_w1),
        "ch_b1": f32(L0_attn_ch_b1).reshape(1, 256),
        "ch_w2": f32(L0_attn_ch_w2),
        "ch_b2": f32(L0_attn_ch_b2).reshape(1, C),
        "proj_w": f32(L0_attn_proj_w),
        "proj_b": f32(L0_attn_proj_b).reshape(1, C),
        "w_fc": f32(L0_ffn_w_fc),
        "w_proj": f32(L0_ffn_w_proj),
        "uT": np.ascontiguousarray(f32(L0_ffn_u).T),
        "v": f32(L0_ffn_v),
        "fbT": np.ascontiguousarray(f32(L0_ffn_fast_b).T),
        "smalls": smalls,
    }
    if N not in _NC_CACHE:
        _NC_CACHE[N] = build(N)
    nc = _NC_CACHE[N]
    in_maps = [dict(common, x=np.ascontiguousarray(x[i])) for i in range(B)]
    res = run_bass_kernel_spmd(nc, in_maps, core_ids=list(range(B)))
    return np.stack([np.asarray(r["y"], dtype=np.float32) for r in res.results], axis=0)


# version 14 (final)
# speedup vs baseline: 1.0122x; 1.0122x over previous
import numpy as np
import contextlib
import concourse.bass as bass
import concourse.mybir as mybir
from concourse.bass_utils import run_bass_kernel_spmd
from concourse.alu_op_type import AluOpType as A

F32 = mybir.dt.float32
BF16 = mybir.dt.bfloat16
AF = mybir.ActivationFunctionType
AX = mybir.AxisListType.X

C = 1024
H = 4096
CH = 8
HCH = 32
EPS = 1e-5

S_C1W, S_C2W, S_GNG, S_GNB, S_C2B, S_C1, S_C2, S_GATE, S_N = 0, 12, 24, 28, 32, 33, 34, 35, 40


def build(N=8192, phases=5):
    NS = N // 128
    NT = N // 256
    import os as _os
    nc = bass.Bass("TRN2", target_bir_lowering=False,
                   detect_race_conditions=bool(_os.environ.get("BASS_RACE_CHECK")))

    x_h = nc.dram_tensor("x", [N, C], F32, kind="ExternalInput").ap()
    qkvw_h = nc.dram_tensor("qkv_w", [C, 3 * C], F32, kind="ExternalInput").ap()
    qkvwT_h = nc.dram_tensor("qkv_wT", [3 * C, C], F32, kind="ExternalInput").ap()
    chw1_h = nc.dram_tensor("ch_w1", [C, 256], F32, kind="ExternalInput").ap()
    chb1_h = nc.dram_tensor("ch_b1", [1, 256], F32, kind="ExternalInput").ap()
    chw2_h = nc.dram_tensor("ch_w2", [256, C], F32, kind="ExternalInput").ap()
    chb2_h = nc.dram_tensor("ch_b2", [1, C], F32, kind="ExternalInput").ap()
    projw_h = nc.dram_tensor("proj_w", [C, C], F32, kind="ExternalInput").ap()
    projb_h = nc.dram_tensor("proj_b", [1, C], F32, kind="ExternalInput").ap()
    wfc_h = nc.dram_tensor("w_fc", [H, C], F32, kind="ExternalInput").ap()
    wproj_h = nc.dram_tensor("w_proj", [C, H], F32, kind="ExternalInput").ap()
    uT_h = nc.dram_tensor("uT", [16, H], F32, kind="ExternalInput").ap()
    v_h = nc.dram_tensor("v", [16, C], F32, kind="ExternalInput").ap()
    fbT_h = nc.dram_tensor("fbT", [16, 16], F32, kind="ExternalInput").ap()
    smalls_h = nc.dram_tensor("smalls", [1, S_N], F32, kind="ExternalInput").ap()
    a_hb = nc.dram_tensor("a_scr", [N, C], BF16, kind="Internal").ap()
    x1_hb = nc.dram_tensor("x1_scr", [N, C], F32, kind="Internal").ap()
    f_hb = nc.dram_tensor("f_scr", [N, C], BF16, kind="Internal").ap()
    w1scr = nc.dram_tensor("w1_scr", [H, C], BF16, kind="Internal").ap()
    w2scr = nc.dram_tensor("w2_scr", [C, H], BF16, kind="Internal").ap()
    y_h = nc.dram_tensor("y", [N, C], F32, kind="ExternalOutput").ap()
    dbg_h = nc.dram_tensor("dbg", [N, C], F32, kind="Internal").ap()

    st0 = contextlib.ExitStack()
    G = st0.enter_context
    sem = {k: G(nc.semaphore("sem_" + k))
           for k in ("dX", "dG0", "dG1", "dG2", "dA0", "dA1", "dA2", "dT0", "dT1",
                     "dU0", "dU1", "dV0", "dW0", "dW1", "dZ0", "dZ1", "dZ2", "dB0", "dB1",
                     "dC", "dG3", "dY0", "dY1", "sT", "sV", "sS", "sG")}
    cnt = {k: 0 for k in sem}

    def inc(instr, s, n=1):
        instr.then_inc(sem[s], n)
        cnt[s] += n
        return cnt[s]

    def wge(eng, s, v):
        if barrier_mode[0] and s in ("sS", "sV", "sT"):
            bar(eng)
            return
        if v and v > 0:
            eng.wait_ge(sem[s], v)

    TE, V, S, GP, SY = nc.tensor, nc.vector, nc.scalar, nc.gpsimd, nc.sync

    barrier_mode = [False]

    def bar(eng):
        eng.wait_ge(sem["sS"], cnt["sS"])
        eng.wait_ge(sem["sV"], cnt["sV"])
        eng.wait_ge(sem["sT"], cnt["sT"])

    def wall(eng):
        for k in sem:
            if k.startswith("d") and cnt[k]:
                eng.wait_ge(sem[k], cnt[k])

    def vsy(instr):
        instr.then_inc(sem["sV"], 1)
        cnt["sV"] += 1
        nc.vector.wait_ge(sem["sV"], cnt["sV"])
        return cnt["sV"]

    def ssy(instr):
        instr.then_inc(sem["sS"], 1)
        cnt["sS"] += 1
        nc.scalar.wait_ge(sem["sS"], cnt["sS"])
        return cnt["sS"]

    Wvp = G(nc.sbuf_tensor("Wvp", [128, CH * C], BF16))
    pb_rep = G(nc.sbuf_tensor("pb_rep", [128, C], F32))
    ident_b = G(nc.sbuf_tensor("ident_b", [128, 128], BF16))
    ident_f = G(nc.sbuf_tensor("ident_f", [128, 128], F32))
    ones_row_f = G(nc.sbuf_tensor("ones_row_f", [1, 128], F32))
    ones_col_f = G(nc.sbuf_tensor("ones_col_f", [128, 1], F32))
    ones_col_b = G(nc.sbuf_tensor("ones_col_b", [128, 1], BF16))
    one_one_f = G(nc.sbuf_tensor("one_one_f", [1, 1], F32))
    one_one_b = G(nc.sbuf_tensor("one_one_b", [1, 1], BF16))
    smalls_row = G(nc.sbuf_tensor("smalls_row", [1, S_N], F32))
    smalls = G(nc.sbuf_tensor("smalls_bc", [128, S_N], F32))
    dumpS = G(nc.sbuf_tensor("dumpS", [128, C], BF16))
    V2T = G(nc.sbuf_tensor("V2T", [16, C], BF16))
    uT_g = G(nc.sbuf_tensor("uT_g", [16, H], BF16))
    weights_a = G(nc.sbuf_tensor("weights_a", [128, NS], F32))
    canary = G(nc.sbuf_tensor("canary", [128, 16], F32))

    def scol(i):
        return smalls[:, i:i + 1]

    st1 = contextlib.ExitStack()
    P1 = st1.enter_context
    Wqk = P1(nc.sbuf_tensor("Wqk", [128, CH * 2048], BF16))
    ch1w = P1(nc.sbuf_tensor("ch1w", [128, CH * 256], BF16))
    ch2w = P1(nc.sbuf_tensor("ch2w", [128, 2 * C], BF16))
    chb1_r = P1(nc.sbuf_tensor("chb1_r", [1, 256], F32))
    chb2_r = P1(nc.sbuf_tensor("chb2_r", [1, C], F32))
    stats = P1(nc.sbuf_tensor("stats", [128, 5 * NS], F32))
    m_acc = P1(nc.sbuf_tensor("m_acc", [128, C], F32))
    mid = P1(nc.sbuf_tensor("midpool", [128, 16 * NS], F32))
    shp1 = P1(nc.sbuf_tensor("shp1", [128, 128], F32))
    shm1 = P1(nc.sbuf_tensor("shm1", [128, 128], F32))
    car_m = P1(nc.sbuf_tensor("car_m", [128, 128], F32))
    car_p = P1(nc.sbuf_tensor("car_p", [128, 128], F32))
    iota = P1(nc.sbuf_tensor("iota", [128, 128], F32))
    ones_mat_f = P1(nc.sbuf_tensor("ones_mat_f", [128, 128], F32))
    dumpV = P1(nc.sbuf_tensor("dumpV", [128, C], BF16))


    st_su = contextlib.ExitStack()
    SU = st_su.enter_context
    wtmp = [SU(nc.sbuf_tensor(f"wtmp{i}", [128, 2048], F32)) for i in range(2)]
    uT_f = SU(nc.sbuf_tensor("uT_f", [16, 2048], F32))
    v_f = SU(nc.sbuf_tensor("v_f", [16, C], F32))
    v_b = SU(nc.sbuf_tensor("v_b", [16, C], BF16))
    fbT_f = SU(nc.sbuf_tensor("fbT_f", [16, 16], F32))
    fbT_b = SU(nc.sbuf_tensor("fbT_b", [16, 16], BF16))
    V2b = SU(nc.sbuf_tensor("V2b", [128, 16], BF16))
    pbrow = SU(nc.sbuf_tensor("pbrow", [1, C], F32))
    ps_a = SU(nc.psum_tensor("ps_su_a", [128, 512], F32))
    ps_b = SU(nc.psum_tensor("ps_su_b", [128, 512], F32))
    ps_c = SU(nc.psum_tensor("ps_su_c", [128, 128], F32))
    ps_cb = SU(nc.psum_tensor("ps_su_cb", [16, 128], BF16))

    inc(SY.dma_start(smalls_row[:, :], smalls_h[:, :]), "dX", 16)
    inc(SY.dma_start(uT_f[:, :], uT_h[:, 0:2048]), "dX", 16)
    inc(SY.dma_start(v_f[:, :], v_h[:, :]), "dX", 16)
    inc(SY.dma_start(fbT_f[:, :], fbT_h[:, :]), "dX", 16)
    ev = inc(SY.dma_start(pbrow[:, :], projb_h[:, :]), "dX", 16)
    ev_small = ev

    GP.memset(ones_row_f[:, :], 1.0)
    GP.memset(ones_col_f[:, :], 1.0)
    GP.memset(ones_col_b[:, :], 1.0)
    GP.memset(ones_mat_f[:, :], 1.0)
    GP.memset(one_one_f[:, :], 1.0)
    GP.memset(one_one_b[:, :], 1.0)
    ev_gp0 = inc(GP.iota(iota[:, :], pattern=[[1, 128]], base=0, channel_multiplier=-1,
                         allow_small_or_imprecise_dtypes=True), "sG", 1)
    wge(V, "sG", ev_gp0)
    V.tensor_single_scalar(ident_b[:, :], iota[:, :], 0.0, A.is_equal)
    V.tensor_single_scalar(ident_f[:, :], iota[:, :], 0.0, A.is_equal)
    wge(V, "dX", ev_small)
    V.tensor_copy(v_b[:, :], v_f[:, :])
    ev_v0 = inc(V.tensor_copy(fbT_b[:, :], fbT_f[:, :]), "sV")

    wge(TE, "dX", ev_small)
    wge(TE, "sV", ev_v0)
    TE.matmul(ps_a[:, 0:S_N], ones_row_f[:, :], smalls_row[:, :], start=True, stop=True)
    ev_t = inc(TE.matmul(ps_b[:, 0:512], ones_row_f[:, :], pbrow[:, 0:512], start=True, stop=True), "sT")
    wge(S, "sT", ev_t)
    S.activation(smalls[:, :], ps_a[:, 0:S_N], AF.Copy)
    ev_s = inc(S.activation(pb_rep[:, 0:512], ps_b[:, 0:512], AF.Copy), "sS")
    wge(TE, "sS", ev_s)
    ev_t = inc(TE.matmul(ps_b[:, 0:512], ones_row_f[:, :], pbrow[:, 512:1024], start=True, stop=True), "sT")
    wge(S, "sT", ev_t)
    ev_pb = inc(S.activation(pb_rep[:, 512:1024], ps_b[:, 0:512], AF.Copy), "sS")
    wge(V, "sS", ev_pb)
    ev_h1 = inc(V.tensor_scalar(uT_g[:, 0:2048], uT_f[:, :], smalls[0:16, S_GATE:S_GATE + 1], None, A.mult), "sV")
    wge(SY, "sV", ev_h1)
    ev_l2 = inc(SY.dma_start(uT_f[:, :], uT_h[:, 2048:4096]), "dX", 16)
    inc(SY.dma_start(chb1_r[:, :], chb1_h[:, :]), "dX", 16)
    inc(SY.dma_start(chb2_r[:, :], chb2_h[:, :]), "dX", 16)
    wge(V, "dX", cnt["dX"])
    ev_ug = inc(V.tensor_scalar(uT_g[:, 2048:4096], uT_f[:, :], smalls[0:16, S_GATE:S_GATE + 1], None, A.mult), "sV")
    for ci in range(CH):
        wge(TE, "sV", ev_v0 if ci == 0 else ev_v2e)
        ev_t = inc(TE.matmul(ps_c[:, 0:16], v_b[:, ci * 128:(ci + 1) * 128], fbT_b[:, :],
                             start=True, stop=True), "sT")
        wge(S, "sT", ev_t)
        ev_s = inc(S.activation(V2b[:, :], ps_c[:, 0:16], AF.Copy), "sS")
        wge(TE, "sS", ev_s)
        ev_t = inc(TE.transpose(ps_cb[:, :], V2b[:, :], ident_b[:, :]), "sT")
        wge(S, "sT", ev_t)
        ev_v2e = inc(S.activation(V2T[0:16, ci * 128:(ci + 1) * 128], ps_cb[:, :], AF.Copy), "sS")

    wge(V, "sG", ev_gp0)
    V.tensor_single_scalar(shp1[:, :], iota[:, :], 1.0, A.is_equal)
    V.tensor_single_scalar(shm1[:, :], iota[:, :], -1.0, A.is_equal)
    V.tensor_single_scalar(car_m[:, :], iota[:, :], -127.0, A.is_equal)
    V.tensor_single_scalar(car_p[:, :], iota[:, :], 127.0, A.is_equal)

    ev_ld = {}
    ev_cast = {}
    loads = [("qk", ci) for ci in range(CH)] + [("c1", ci) for ci in range(CH)] + [("c2", ci) for ci in range(2)]
    for idx, (kind, ci) in enumerate(loads):
        b = idx % 2
        if idx >= 2:
            wge(SY, "sV", ev_cast[idx - 2])
        if kind == "qk":
            src, w = qkvw_h[ci * 128:(ci + 1) * 128, 0:2048], 2048
        elif kind == "c1":
            src, w = chw1_h[ci * 128:(ci + 1) * 128, :], 256
        else:
            src, w = chw2_h[ci * 128:(ci + 1) * 128, :], C
        ev_ld[idx] = inc(SY.dma_start(wtmp[b][:, 0:w], src), f"dW{b}", 16)
        wge(V, f"dW{b}", ev_ld[idx])
        if kind == "qk":
            e = V.tensor_copy(Wqk[:, ci * 2048:(ci + 1) * 2048], wtmp[b][:, 0:2048])
        elif kind == "c1":
            e = V.tensor_copy(ch1w[:, ci * 256:(ci + 1) * 256], wtmp[b][:, 0:256])
        else:
            e = V.tensor_copy(ch2w[:, ci * C:(ci + 1) * C], wtmp[b][:, 0:C])
        ev_cast[idx] = inc(e, "sV")
    ev_setup_v = cnt["sV"]
    st_su.close()
    if phases <= 1:
        wall(SY)
        st1.close()
        st0.close()
        return nc

    x0_sb = [P1(nc.sbuf_tensor(f"x0_{i}", [128, 1024], F32)) for i in range(3)]
    a_sb = [P1(nc.sbuf_tensor(f"a_{i}", [128, 1024], BF16)) for i in range(3)]
    aT_sb = [P1(nc.sbuf_tensor(f"aT_{i}", [128, 8 * 256], BF16, is_transpose_dest=True)) for i in range(2)]
    qk_sb = [P1(nc.sbuf_tensor(f"qk_{i}", [128, 2048], BF16)) for i in range(2)]
    p1c = [P1(nc.sbuf_tensor(f"p1c_{i}", [128, 12], F32)) for i in range(2)]
    st_p1p = contextlib.ExitStack()
    qp = [st_p1p.enter_context(nc.psum_tensor(f"qp{j}", [128, 1024], F32)) for j in range(3)]

    ev_x0, ev_aw, ev_at, ev_apply, ev_qkv, ev_qke, ev_vstat, ev_ssq = {}, {}, {}, {}, {}, {}, {}, {}
    ev_tpe2 = {}
    ev_tp, ev_tpe, ev_psfree = {}, {}, {}
    ev_ln1, ev_ma = {}, {}
    V.memset(m_acc[:, :], 0.0)

    def p1_load(t):
        if t >= NT:
            return
        for j in range(2):
            u = 2 * t + j
            if u >= 3:
                wge(GP, "sV", max(ev_apply[u - 3], ev_ln1.get(u - 3, 0)))
            ev_x0[u] = inc(GP.dma_start(x0_sb[u % 3][:, :], x_h[u * 128:(u + 1) * 128, :]),
                           f"dA{u % 3}", 16)

    def p1_ln(t):
        if t >= NT:
            return
        for j in range(2):
            u = 2 * t + j
            x0v = x0_sb[u % 3][:, :]
            pc = p1c[u % 2]
            wge(V, f"dA{u % 3}", ev_x0[u])
            vsy(V.reduce_sum(pc[:, 0:1], x0v, axis=AX))
            ev_ln1[u] = vsy(V.scalar_tensor_tensor(dumpV[:, :], x0v, 1.0, x0v, A.mult, A.mult,
                                                   accum_out=pc[:, 1:2]))
            vsy(V.tensor_scalar(pc[:, 2:3], pc[:, 0:1], 1.0 / C, None, A.mult))
            vsy(V.tensor_scalar(pc[:, 3:4], pc[:, 1:2], 1.0 / C, EPS, A.mult, A.add))
            vsy(V.tensor_tensor(pc[:, 4:5], pc[:, 2:3], pc[:, 2:3], A.mult))
            e1 = vsy(V.tensor_tensor(pc[:, 5:6], pc[:, 3:4], pc[:, 4:5], A.subtract))
            wge(S, "sV", e1)
            e2 = inc(S.activation(pc[:, 6:7], pc[:, 5:6], AF.Sqrt), "sS")
            wge(V, "sS", e2)
            vsy(V.reciprocal(pc[:, 7:8], pc[:, 6:7]))
            e3 = vsy(V.scalar_tensor_tensor(pc[:, 8:9], pc[:, 2:3], -1.0, pc[:, 7:8], A.mult, A.mult))
            wge(S, "sV", e3)
            if u >= 3:
                wge(V, f"dG{u % 3}", ev_aw.get(u - 3, 0))
            ev_apply[u] = inc(V.tensor_scalar(a_sb[u % 3][:, :], x0v, pc[:, 7:8], pc[:, 8:9],
                                              A.mult, A.add), "sV")
            ev_ma[u] = inc(V.tensor_tensor(m_acc[:, :], m_acc[:, :], a_sb[u % 3][:, :], A.add), "sV")
            wge(GP, "sV", ev_apply[u])
            ev_aw[u] = inc(GP.dma_start(a_hb[u * 128:(u + 1) * 128, :], a_sb[u % 3][:, :]),
                           f"dG{u % 3}", 16)
        for q_, rng, semk, evd in ((SY, range(4), f"dT{t % 2}", ev_tpe),
                                   (S, range(4, CH), f"dU{t % 2}", ev_tpe2)):
            wge(q_, f"dG{(2 * t) % 3}", ev_aw[2 * t])
            wge(q_, f"dG{(2 * t + 1) % 3}", ev_aw[2 * t + 1])
            if t >= 2:
                wge(q_, "sT", ev_qkv[(t - 2, 1)])
            for ci in rng:
                e = inc(q_.dma_start(aT_sb[t % 2][:, ci * 256:(ci + 1) * 256],
                                     a_hb[t * 256:(t + 1) * 256, ci * 128:(ci + 1) * 128],
                                     transpose=True), semk, 16)
            evd[2 * t] = evd[2 * t + 1] = e

    wge(TE, "sS", cnt["sS"])
    wge(TE, "sV", cnt["sV"])
    wge(SY, "sS", cnt["sS"])
    wge(SY, "sV", cnt["sV"])
    wge(SY, "sT", cnt["sT"])
    wge(GP, "sS", cnt["sS"])
    wge(GP, "sV", cnt["sV"])
    p1_load(0)
    p1_ln(0)
    p1_load(1)
    for t in range(NT):
        p1_ln(t + 1)
        p1_load(t + 2)
        for j in range(2):
            u = 2 * t + j
            qi, ki = (2 * u) % 3, (2 * u + 1) % 3
            wge(TE, f"dT{t % 2}", ev_tpe[2 * t + 1])
            wge(TE, f"dU{t % 2}", ev_tpe2[2 * t + 1])
            wge(TE, "sV", ev_setup_v)
            wge(TE, "sS", ev_psfree.get(qi, 0))
            wge(TE, "sS", ev_psfree.get(ki, 0))
            for ci in range(CH):
                for n in range(4):
                    pj = qp[qi] if n < 2 else qp[ki]
                    e = TE.matmul(pj[:, (n % 2) * 512:(n % 2 + 1) * 512],
                                  aT_sb[t % 2][:, ci * 256 + j * 128: ci * 256 + (j + 1) * 128],
                                  Wqk[:, ci * 2048 + n * 512: ci * 2048 + (n + 1) * 512],
                                  start=(ci == 0), stop=(ci == CH - 1))
            ev_qkv[(t, j)] = inc(e, "sT")
            wge(S, "sT", ev_qkv[(t, j)])
            if u >= 2:
                wge(S, "sV", ev_vstat[u - 2])
            S.activation(qk_sb[u % 2][:, 0:1024], qp[qi][:, :], AF.Copy,
                         accum_out=stats[:, 0 * NS + u:0 * NS + u + 1])
            ev_qke[u] = inc(S.activation(qk_sb[u % 2][:, 1024:2048], qp[ki][:, :],
                                         AF.Copy, accum_out=stats[:, 1 * NS + u:1 * NS + u + 1]), "sS")
            S.activation(dumpS[:, :], qp[qi][:, :], AF.Square,
                         accum_out=stats[:, 2 * NS + u:2 * NS + u + 1])
            ev_ssq[u] = inc(S.activation(dumpS[:, :], qp[ki][:, :], AF.Square,
                                         accum_out=stats[:, 3 * NS + u:3 * NS + u + 1]), "sS")
            ev_psfree[qi] = ev_ssq[u]
            ev_psfree[ki] = ev_ssq[u]
            wge(V, "sS", ev_qke[u])
            ev_vstat[u] = inc(V.scalar_tensor_tensor(
                dumpV[:, :], qk_sb[u % 2][:, 0:1024], 1.0, qk_sb[u % 2][:, 1024:2048],
                A.mult, A.mult, accum_out=stats[:, 4 * NS + u:4 * NS + u + 1]), "sV")

    ev_p1_s = cnt["sS"]
    ev_p1_v = cnt["sV"]
    st_p1p.close()
    if phases == 21:
        wge(SY, "sS", cnt["sS"])
        wge(SY, "sV", cnt["sV"])
        inc(SY.dma_start(y_h[128:256, 0:5 * NS], stats[:, :]), "dX", 16)
        inc(SY.dma_start(y_h[0:128, 0:C], m_acc[:, :]), "dX", 16)
        SY.wait_ge(sem["dX"], cnt["dX"])
        st1.close()
        st0.close()
        return nc
    if phases == 22:
        wge(SY, "sS", cnt["sS"])
        wge(SY, "sV", cnt["sV"])
        wge(SY, "sT", cnt["sT"])
        for u in range(NS):
            e = inc(SY.dma_start(a_sb[0][:, :], a_hb[u * 128:(u + 1) * 128, :]), "dX", 16)
            wge(S, "dX", e)
            es = inc(S.activation(x0_sb[0][:, :], a_sb[0][:, :], AF.Copy), "sS")
            wge(SY, "sS", es)
            e2 = inc(SY.dma_start(y_h[u * 128:(u + 1) * 128, :], x0_sb[0][:, :]), "dX", 16)
            wge(SY, "dX", e2)
        SY.wait_ge(sem["dX"], cnt["dX"])
        st1.close()
        st0.close()
        return nc
    if phases <= 2:
        wall(SY)
        st1.close()
        st0.close()
        return nc

    st_mp = contextlib.ExitStack()
    MP = st_mp.enter_context
    ps_m = MP(nc.psum_tensor("ps_m", [128, 16], F32))
    ps_r1 = MP(nc.psum_tensor("ps_r1", [1, 512], F32))
    ps_r2 = MP(nc.psum_tensor("ps_r2", [1, 512], F32))
    ps_bc = MP(nc.psum_tensor("ps_bc", [128, 128], F32))
    ps_sh = MP(nc.psum_tensor("ps_sh", [128, NS], F32))
    ps_sh2 = MP(nc.psum_tensor("ps_sh2", [128, NS], F32))
    m_accb = MP(nc.sbuf_tensor("m_accb", [128, C], BF16))
    m_cols = MP(nc.sbuf_tensor("m_cols", [128, CH], BF16))
    ch_cols = MP(nc.sbuf_tensor("ch_cols", [128, CH], F32))
    gc = MP(nc.sbuf_tensor("gcols", [128, 48], F32))
    grow = MP(nc.sbuf_tensor("grow", [1, 128], F32))
    h1rowb = MP(nc.sbuf_tensor("h1rowb", [1, 256], BF16))
    h1colb = MP(nc.sbuf_tensor("h1colb", [128, 2], BF16))
    chrow = MP(nc.sbuf_tensor("chrow", [1, C], F32))

    def m_arr(i):
        return mid[:, i * NS:(i + 1) * NS]

    barrier_mode[0] = False
    if phases == 31:
        wge(SY, "sS", cnt["sS"])
        wge(SY, "sV", cnt["sV"])
        inc(SY.dma_start(y_h[384:512, 0:5 * NS], stats[:, :]), "dX", 16)
    V.memset(gc[:, :], 0.0)
    V.memset(grow[:, :], 0.0)
    wge(V, "sV", ev_p1_v)
    ev = inc(V.tensor_copy(m_accb[:, :], m_acc[:, :]), "sV")
    wge(TE, "sV", ev)
    for ci in range(CH):
        e = TE.matmul(ps_m[:, ci:ci + 1], m_accb[:, ci * 128:(ci + 1) * 128], ones_col_b[:, :],
                      start=(ci == 0), stop=(ci == CH - 1))
    ev = inc(e, "sT")
    wge(S, "sT", ev)
    ev = inc(S.activation(m_cols[:, 0:CH], ps_m[:, 0:CH], AF.Copy, scale=1.0 / N), "sS")
    wge(TE, "sS", ev)
    for ci in range(CH):
        e = TE.matmul(ps_r1[:, 0:256], m_cols[:, ci:ci + 1], ch1w[:, ci * 256:(ci + 1) * 256],
                      start=(ci == 0), stop=(ci == CH - 1))
    ev = inc(e, "sT")
    wge(V, "sT", ev)
    vsy(V.tensor_tensor(grow[:, 0:128], ps_r1[0:1, 0:128], chb1_r[:, 0:128], A.add))
    vsy(V.tensor_tensor(chrow[:, 0:128], ps_r1[0:1, 128:256], chb1_r[:, 128:256], A.add))
    vsy(V.tensor_scalar_max(h1rowb[:, 0:128], grow[:, 0:128], 0.0))
    ev = vsy(V.tensor_scalar_max(h1rowb[:, 128:256], chrow[:, 0:128], 0.0))
    wge(TE, "sV", ev)
    for s_ in range(2):
        e = TE.matmul(ps_m[:, 8 + s_:9 + s_], h1rowb[0:1, s_ * 128:(s_ + 1) * 128], one_one_b[:, :],
                      start=(s_ == 0), stop=(s_ == 1))
    ev = inc(e, "sT")
    wge(S, "sT", ev)
    ev = inc(S.activation(h1colb[:, :], ps_m[:, 8:10], AF.Copy), "sS")
    wge(TE, "sS", ev)
    for s_ in range(2):
        for n in range(2):
            psx = ps_r1 if n == 0 else ps_r2
            e = TE.matmul(psx[:, 0:512], h1colb[:, s_:s_ + 1],
                          ch2w[:, s_ * C + n * 512: s_ * C + (n + 1) * 512],
                          start=(s_ == 0), stop=(s_ == 1))
    ev = inc(e, "sT")
    wge(V, "sT", ev)
    vsy(V.tensor_tensor(chrow[:, 0:512], ps_r1[0:1, 0:512], chb2_r[:, 0:512], A.add))
    ev_chpre = vsy(V.tensor_tensor(chrow[:, 512:1024], ps_r2[0:1, 0:512], chb2_r[:, 512:1024], A.add))

    sqv, skv, sqqv, skkv, sqkv = (stats[:, i * NS:(i + 1) * NS] for i in range(5))
    muq, muk, sgqk, sgq2, sgk2, num, den, base = (m_arr(i) for i in range(8))
    wge(V, "sS", ev_p1_s)
    vsy(V.tensor_scalar(muq, sqv, 1.0 / C, None, A.mult))
    vsy(V.tensor_scalar(muk, skv, 1.0 / C, None, A.mult))
    vsy(V.tensor_tensor(sgqk, sqv, skv, A.mult))
    vsy(V.scalar_tensor_tensor(sgqk, sgqk, -1.0 / C, sqkv, A.mult, A.add))
    vsy(V.tensor_scalar(sgqk, sgqk, 1.0 / (C - 1), None, A.mult))
    vsy(V.tensor_tensor(sgq2, sqv, sqv, A.mult))
    vsy(V.scalar_tensor_tensor(sgq2, sgq2, -1.0 / C, sqqv, A.mult, A.add))
    vsy(V.tensor_scalar(sgq2, sgq2, 1.0 / (C - 1), None, A.mult))
    vsy(V.tensor_tensor(sgk2, skv, skv, A.mult))
    vsy(V.scalar_tensor_tensor(sgk2, sgk2, -1.0 / C, skkv, A.mult, A.add))
    vsy(V.tensor_scalar(sgk2, sgk2, 1.0 / (C - 1), None, A.mult))
    vsy(V.tensor_tensor(num, muq, muk, A.mult))
    vsy(V.tensor_scalar(num, num, 2.0, scol(S_C1), A.mult, A.add))
    vsy(V.tensor_scalar(sgqk, sgqk, 2.0, scol(S_C2), A.mult, A.add))
    vsy(V.tensor_tensor(num, num, sgqk, A.mult))
    vsy(V.tensor_tensor(den, muq, muq, A.mult))
    vsy(V.tensor_tensor(muk, muk, muk, A.mult))
    vsy(V.tensor_tensor(den, den, muk, A.add))
    vsy(V.tensor_scalar(den, den, scol(S_C1), None, A.add))
    vsy(V.tensor_tensor(sgq2, sgq2, sgk2, A.add))
    vsy(V.tensor_scalar(sgq2, sgq2, scol(S_C2), None, A.add))
    vsy(V.tensor_tensor(den, den, sgq2, A.mult))
    vsy(V.tensor_scalar(den, den, 1e-7, None, A.add))
    vsy(V.reciprocal(den, den))
    vsy(V.tensor_tensor(num, num, den, A.mult))
    ev_base = vsy(V.tensor_tensor(base, num, num, A.mult))

    am1, ap1, hc0, hc1, hc2, hc3, acc, A1 = (m_arr(i) for i in range(8, 16))
    hcs = (hc0, hc1, hc2, hc3)

    def shift(dst, src, evw, dirn, psum):
        wge(TE, "sV", evw)
        if dirn == "m1":
            TE.matmul(psum[:, :], shp1[:, :], src, start=True, stop=False)
            e = TE.matmul(psum[:, 1:NS], car_m[:, :], src[:, 0:NS - 1], start=False, stop=True)
        else:
            TE.matmul(psum[:, :], shm1[:, :], src, start=True, stop=False)
            e = TE.matmul(psum[:, 0:NS - 1], car_p[:, :], src[:, 1:NS], start=False, stop=True)
        e = inc(e, "sT")
        wge(S, "sT", e)
        return inc(S.activation(dst, psum[:, :], AF.Copy), "sS")

    e1 = shift(am1, base, ev_base, "m1", ps_sh)
    e2 = shift(ap1, base, ev_base, "p1", ps_sh2)
    wge(V, "sS", e2)
    for c in range(4):
        vsy(V.tensor_scalar(hcs[c], base, scol(S_C1W + c * 3 + 1), None, A.mult))
        vsy(V.scalar_tensor_tensor(hcs[c], am1, scol(S_C1W + c * 3 + 0), hcs[c], A.mult, A.add))
        vsy(V.scalar_tensor_tensor(hcs[c], ap1, scol(S_C1W + c * 3 + 2), hcs[c], A.mult, A.add))
    for c in range(4):
        ev = vsy(V.reduce_sum(gc[:, c:c + 1], hcs[c], axis=AX))
    wge(S, "sV", ev)
    for c in range(4):
        e = S.activation(dumpS[:, 0:NS], hcs[c], AF.Square, accum_out=gc[:, 4 + c:5 + c])
    ev = inc(e, "sS")
    wge(V, "sS", ev)
    vsy(V.tensor_tensor(gc[:, 8:9], gc[:, 0:1], gc[:, 1:2], A.add))
    vsy(V.tensor_tensor(gc[:, 9:10], gc[:, 2:3], gc[:, 3:4], A.add))
    vsy(V.tensor_tensor(gc[:, 10:11], gc[:, 4:5], gc[:, 5:6], A.add))
    ev = vsy(V.tensor_tensor(gc[:, 11:12], gc[:, 6:7], gc[:, 7:8], A.add))
    wge(TE, "sV", ev)
    ev = inc(TE.matmul(ps_bc[:, 0:4], ones_mat_f[:, :], gc[:, 8:12], start=True, stop=True), "sT")
    wge(S, "sT", ev)
    ev = inc(S.activation(gc[:, 20:24], ps_bc[:, 0:4], AF.Copy), "sS")
    wge(V, "sS", ev)
    vsy(V.tensor_scalar(gc[:, 24:26], gc[:, 20:22], 1.0 / (2 * N), None, A.mult))
    vsy(V.tensor_scalar(gc[:, 26:28], gc[:, 22:24], 1.0 / (2 * N), EPS, A.mult, A.add))
    vsy(V.tensor_tensor(gc[:, 28:30], gc[:, 24:26], gc[:, 24:26], A.mult))
    ev = vsy(V.tensor_tensor(gc[:, 26:28], gc[:, 26:28], gc[:, 28:30], A.subtract))
    wge(S, "sV", ev)
    ev = inc(S.activation(gc[:, 30:32], gc[:, 26:28], AF.Sqrt), "sS")
    wge(V, "sS", ev)
    vsy(V.reciprocal(gc[:, 26:27], gc[:, 30:31]))
    ev = vsy(V.reciprocal(gc[:, 27:28], gc[:, 31:32]))
    for c in range(4):
        g = c // 2
        vsy(V.tensor_scalar(gc[:, 32 + c:33 + c], gc[:, 26 + g:27 + g], scol(S_GNG + c), None, A.mult))
        vsy(V.tensor_tensor(gc[:, 36 + c:37 + c], gc[:, 24 + g:25 + g], gc[:, 32 + c:33 + c], A.mult))
        ev = vsy(V.scalar_tensor_tensor(gc[:, 36 + c:37 + c], gc[:, 36 + c:37 + c], -1.0,
                                        scol(S_GNB + c), A.mult, A.add))
    wge(S, "sV", ev)
    for c in range(4):
        e = S.activation(hcs[c], hcs[c], AF.Relu, bias=gc[:, 36 + c:37 + c], scale=gc[:, 32 + c:33 + c])
    ev_gn = inc(e, "sS")
    wge(V, "sS", ev_gn)
    evv = vsy(V.memset(acc[:, :], 0.0))
    for c in range(4):
        e1 = shift(am1, hcs[c], evv, "m1", ps_sh)
        e2 = shift(ap1, hcs[c], 0, "p1", ps_sh2)
        wge(V, "sS", e2)
        vsy(V.scalar_tensor_tensor(acc, am1, scol(S_C2W + c * 3 + 0), acc, A.mult, A.add))
        vsy(V.scalar_tensor_tensor(acc, hcs[c], scol(S_C2W + c * 3 + 1), acc, A.mult, A.add))
        evv = vsy(V.scalar_tensor_tensor(acc, ap1, scol(S_C2W + c * 3 + 2), acc, A.mult, A.add))
    vsy(V.tensor_scalar(acc, acc, scol(S_C2B), None, A.add))
    ev_a1 = vsy(V.tensor_tensor(A1, base, acc, A.add))
    wge(S, "sV", max(ev_a1, ev_chpre))
    AF_SIG = AF.Copy if phases == 35 else AF.Sigmoid
    S.activation(chrow[:, :], chrow[:, :], AF_SIG)
    ev_sig = ssy(S.activation(A1, A1, AF_SIG))
    wge(TE, "sS", ev_sig)
    for ci in range(CH):
        e = TE.matmul(ps_m[:, ci:ci + 1], chrow[0:1, ci * 128:(ci + 1) * 128], one_one_f[:, :],
                      start=(ci == 0), stop=(ci == CH - 1))
    ev = inc(e, "sT")
    wge(S, "sT", ev)
    ev_chc = inc(S.activation(ch_cols[:, :], ps_m[:, 0:CH], AF.Copy), "sS")
    ev_exp = inc(S.activation(A1, A1, AF.Copy if phases == 35 else AF.Exp, accum_out=gc[:, 3:4]), "sS")
    wge(TE, "sS", ev_exp)
    ev = inc(TE.matmul(ps_bc[:, 4:5], ones_mat_f[:, :], gc[:, 3:4], start=True, stop=True), "sT")
    wge(S, "sT", ev)
    ev = inc(S.activation(gc[:, 4:5], ps_bc[:, 4:5], AF.Copy), "sS")
    wge(V, "sS", ev)
    ev = vsy(V.reciprocal(gc[:, 5:6], gc[:, 4:5]))
    ev_wts = vsy(V.tensor_scalar(weights_a[:, :], A1, gc[:, 5:6], None, A.mult))

    wvt_all = MP(nc.sbuf_tensor("wvt_all", [128, CH * C], BF16))
    ppb = MP(nc.sbuf_tensor("ppb", [128, CH * C], BF16))
    pwf = [MP(nc.sbuf_tensor(f"pwf_{i}", [128, C], F32)) for i in range(2)]
    ps_vp = MP(nc.psum_tensor("ps_vp", [128, C], F32))
    ev_prep = {}
    wge(SY, "sS", cnt["sS"])
    wge(SY, "sV", cnt["sV"])
    wge(SY, "sT", cnt["sT"])
    for cv in range(CH):
        b = cv % 2
        if cv >= 2:
            wge(SY, "sV", ev_prep[cv - 2])
        ev_l1 = inc(SY.dma_start(pwf[b][:, :], projw_h[cv * 128:(cv + 1) * 128, :]), f"dW{b}", 16)
        wge(V, f"dW{b}", ev_l1)
        wge(V, "sS", ev_chc)
        ev_p = inc(V.tensor_scalar(ppb[:, cv * C:(cv + 1) * C], pwf[b][:, :],
                                   ch_cols[:, cv:cv + 1], None, A.mult), "sV")
        wge(SY, "sV", ev_p)
        ev_l2 = inc(SY.dma_start(pwf[b][:, :], qkvwT_h[2048 + cv * 128:2048 + (cv + 1) * 128, :]),
                    f"dW{b}", 16)
        wge(V, f"dW{b}", ev_l2)
        ev_prep[cv] = inc(V.tensor_copy(wvt_all[:, cv * C:(cv + 1) * C], pwf[b][:, :]), "sV")
    ev_vpe = {}
    for mi in range(CH):
        if mi >= 1:
            wge(TE, "sS", ev_vpe[mi - 1])
        for cv in range(CH):
            wge(TE, "sV", ev_prep[cv])
            for n in range(2):
                e = TE.matmul(ps_vp[:, n * 512:(n + 1) * 512],
                              wvt_all[:, cv * C + mi * 128: cv * C + (mi + 1) * 128],
                              ppb[:, cv * C + n * 512: cv * C + (n + 1) * 512],
                              start=(cv == 0), stop=(cv == CH - 1))
        ev = inc(e, "sT")
        wge(S, "sT", ev)
        ev_vpe[mi] = inc(S.activation(Wvp[:, mi * C:(mi + 1) * C], ps_vp[:, :], AF.Copy), "sS")
    if phases in (33, 34):
        wge(SY, "sV", cnt["sV"])
        inc(SY.dma_start(y_h[256:384, 16:16 + NS], weights_a[:, :]), "dX", 16)
    barrier_mode[0] = False
    ev_mid_done_s = cnt["sS"]
    ev_mid_done_v = cnt["sV"]
    if phases in (31, 33, 35):
        wge(SY, "sV", cnt["sV"])
        wge(SY, "sS", cnt["sS"])
        wge(SY, "sT", cnt["sT"])
        inc(SY.dma_start(y_h[0:128, 0:NS], weights_a[:, :]), "dX", 16)
        inc(SY.dma_start(y_h[0:128, 256:256 + CH], ch_cols[:, :]), "dX", 16)
        inc(SY.dma_start(y_h[0:128, 288:288 + NS], mid[:, 15 * NS:16 * NS]), "dX", 16)
        inc(SY.dma_start(y_h[0:128, 296:296 + NS], mid[:, 7 * NS:8 * NS]), "dX", 16)
        inc(SY.dma_start(y_h[0:128, 304:304 + NS], mid[:, 10 * NS:11 * NS]), "dX", 16)
        inc(SY.dma_start(y_h[0:128, 308:308 + NS], mid[:, 0 * NS:1 * NS]), "dX", 16)
        inc(SY.dma_start(y_h[0:128, 312:312 + NS], mid[:, 5 * NS:6 * NS]), "dX", 16)
        inc(SY.dma_start(y_h[0:128, 316:316 + NS], mid[:, 6 * NS:7 * NS]), "dX", 16)
        inc(SY.dma_start(y_h[0:128, 320:320 + NS], mid[:, 2 * NS:3 * NS]), "dX", 16)
        inc(SY.dma_start(y_h[0:128, 324:324 + 8], smalls[:, 32:40]), "dX", 16)
        inc(SY.dma_start(y_h[0:128, 384:384 + 48], gc[:, :]), "dX", 16)
        inc(SY.dma_start(y_h[0:128, 640:640 + 5 * NS], stats[:, :]), "dX", 16)
        EXIT31 = True
        with nc.sbuf_tensor("dbg_f", [128, 1024], F32) as tmpd, \
             nc.sbuf_tensor("dbg_t", [128, 256], BF16, is_transpose_dest=True) as tmpt:
            ev = inc(S.activation(tmpd[:, :], Wvp[:, 0:1024], AF.Copy), "sS")
            wge(SY, "sS", ev)
            inc(SY.dma_start(y_h[128:256, :], tmpd[:, :]), "dX", 16)
            ev = inc(SY.dma_start(tmpt[:, :], a_hb[0:256, 0:128], transpose=True), "dX", 16)
            wge(S, "dX", ev)
            ev = inc(S.activation(tmpd[:, 0:256], tmpt[:, :], AF.Copy), "sS")
            wge(SY, "sS", ev)
            inc(SY.dma_start(y_h[256:384, 0:256], tmpd[:, 0:256]), "dX", 16)
            SY.wait_ge(sem["dX"], cnt["dX"])
        st_mp.close()
        st1.close()
        st0.close()
        return nc
    st_mp.close()
    st1.close()
    if phases <= 3:
        wall(SY)
        st0.close()
        return nc

    st2 = contextlib.ExitStack()
    P2 = st2.enter_context
    W1p = P2(nc.sbuf_tensor("W1p", [128, CH * H], BF16))
    W2eT = P2(nc.sbuf_tensor("W2eT", [128, HCH * C], BF16, is_transpose_dest=True))
    st_2as = contextlib.ExitStack()
    P2A = st_2as.enter_context
    atp = [P2A(nc.sbuf_tensor(f"atp_{i}", [128, 8 * 256], BF16, is_transpose_dest=True)) for i in range(2)]
    x0p = [P2A(nc.sbuf_tensor(f"x0p_{i}", [128, 1024], F32)) for i in range(3)]
    zw = [P2A(nc.sbuf_tensor(f"zw_{i}", [128, 1024], F32)) for i in range(1)]
    cols2a = P2A(nc.sbuf_tensor("cols2a", [128, 64], F32))
    p2c = [cols2a[:, 0:12], cols2a[:, 12:24]]
    wfc_f = [P2A(nc.sbuf_tensor(f"wfc_{i}", [128, C], F32)) for i in range(1)]
    wps_f = wfc_f
    W1e_g = P2A(nc.sbuf_tensor("W1e_g", [128, 2 * C], BF16))
    w1eT = [P2A(nc.sbuf_tensor(f"w1eT_{i}", [128, 256], BF16, is_transpose_dest=True)) for i in range(2)]
    W2e_sl = P2A(nc.sbuf_tensor("W2e_sl", [128, C], BF16))
    hilo = P2A(nc.sbuf_tensor("hilo", [128, 512], F32))
    hi_s = hilo[:, 0:256]
    lo_s = hilo[:, 256:512]
    bcol = cols2a[:, 24:40]
    wpsc = cols2a[:, 40:64]
    st_2ap = contextlib.ExitStack()
    z_ps = [st_2ap.enter_context(nc.psum_tensor(f"z_ps{j}", [128, C], F32)) for j in range(2)]
    bld_ps = [st_2ap.enter_context(nc.psum_tensor(f"bld_ps{j}", [128, 512], F32)) for j in range(2)]


    ev2_x0, ev2_at, ev2_z, ev2_zwe, ev2_x1, ev2_x1w = {}, {}, {}, {}, {}, {}
    ev2_tpe2 = {}
    ev2_a, ev2_tp, ev2_tpe = {}, {}, {}
    ev_w1e, ev_w1pe, ev_w1w = {}, {}, {}
    ev_wpsc, ev_w2sl, ev_w2te, ev_w2w = {}, {}, {}, {}
    bl_ev = {}
    bl_ctr = [0]

    bld_bufs = [wfc_f[0], x0p[2]]

    def bld_load(src_ap):
        b = bl_ctr[0] % 2
        if bl_ctr[0] >= 2:
            s_, v_ = bl_ev[bl_ctr[0] - 2]
            wge(SY, s_, v_)
        e = inc(SY.dma_start(bld_bufs[b][:, :], src_ap), f"dB{b}", 16)
        bl_ctr[0] += 1
        return b, e

    def bld_done(sem_name, val):
        bl_ev[bl_ctr[0] - 1] = (sem_name, val)

    def p2a_load(t):
        if t >= NT:
            return
        for j in range(2):
            u = 2 * t + j
            if u >= 3:
                wge(GP, f"dG{u % 3}", ev2_x1w[u - 3])
            ev2_x0[u] = inc(GP.dma_start(x0p[u % 3][:, :], x_h[u * 128:(u + 1) * 128, :]),
                            f"dA{u % 3}", 16)
        if t >= 2:
            wge(SY, "sT", ev2_z[(t - 2, 1)])
            wge(S, "sT", ev2_z[(t - 2, 1)])
        for q_, rng, semk, evd in ((SY, range(4), f"dT{t % 2}", ev2_tpe),
                                   (S, range(4, CH), f"dU{t % 2}", ev2_tpe2)):
            for ci in rng:
                e = inc(q_.dma_start(atp[t % 2][:, ci * 256:(ci + 1) * 256],
                                     a_hb[t * 256:(t + 1) * 256, ci * 128:(ci + 1) * 128],
                                     transpose=True), semk, 16)
            evd[2 * t] = evd[2 * t + 1] = e

    def p2a_tp(t):
        pass


    def bld_wfc(kk):
        b, e = bld_load(wfc_h[kk * 128:(kk + 1) * 128, :])
        wge(S, f"dB{b}", e)
        ev = inc(S.activation(dumpS[:, :], bld_bufs[b][:, :], AF.Abs,
                              accum_out=bcol[:, b * 8:b * 8 + 1]), "sS")
        wge(V, "sS", ev)
        vsy(V.tensor_scalar(bcol[:, b * 8 + 1:b * 8 + 2], bcol[:, b * 8:b * 8 + 1], 1.0 / C, 1e-5, A.mult, A.max))
        vsy(V.tensor_scalar(bcol[:, b * 8 + 2:b * 8 + 3], bcol[:, b * 8 + 1:b * 8 + 2], 0.5, None, A.mult))
        vsy(V.tensor_scalar(bcol[:, b * 8 + 3:b * 8 + 4], bcol[:, b * 8 + 1:b * 8 + 2], -0.5, None, A.mult))
        if kk >= 2:
            wge(V, f"dZ{kk % 2}", ev_w1w.get(kk - 2, 0))
        for hf in range(4):
            sl_ = slice(hf * 256, (hf + 1) * 256)
            V.tensor_scalar(hi_s[:, :], bld_bufs[b][:, sl_], bcol[:, b * 8 + 2:b * 8 + 3],
                            bcol[:, b * 8 + 1:b * 8 + 2], A.is_gt, A.mult)
            V.tensor_scalar(lo_s[:, :], bld_bufs[b][:, sl_], bcol[:, b * 8 + 3:b * 8 + 4],
                            bcol[:, b * 8 + 1:b * 8 + 2], A.is_lt, A.mult)
            e = V.tensor_tensor(W1e_g[:, (kk % 2) * C + hf * 256:(kk % 2) * C + (hf + 1) * 256],
                                hi_s[:, :], lo_s[:, :], A.subtract)
        ev_w1e[kk] = inc(e, "sV")
        bld_done("sV", ev_w1e[kk])
        wge(SY, "sV", ev_w1e[kk])
        ew = inc(SY.dma_start(w1scr[kk * 128:(kk + 1) * 128, :], W1e_g[:, (kk % 2) * C:(kk % 2 + 1) * C]),
                 f"dZ{kk % 2}", 16)
        ev_w1w[kk] = ew
        wge(SY, f"dZ{kk % 2}", ew)
        if kk % 2 == 1:
            g = kk // 2
            for ci in range(CH):
                pp = bld_ps[ci % 2]
                if g >= 1 or ci >= 2:
                    pk = (g, ci - 2) if ci >= 2 else (g - 1, CH - 2 + ci)
                    wge(SY, "sV", ev_w1pe[pk])
                et = inc(SY.dma_start(w1eT[ci % 2][:, :],
                                      w1scr[g * 256:(g + 1) * 256, ci * 128:(ci + 1) * 128],
                                      transpose=True), f"dT{ci % 2}", 16)
                wge(TE, "sV", ev_ug)
                if g >= 1 or ci >= 2:
                    wge(TE, "sV", ev_w1pe[pk])
                e = TE.matmul(pp[:, 0:256], V2T[:, ci * 128:(ci + 1) * 128],
                              uT_g[:, g * 256:(g + 1) * 256], start=True, stop=True)
                ev = inc(e, "sT")
                wge(V, "sT", ev)
                wge(V, f"dT{ci % 2}", et)
                ev_w1pe[(g, ci)] = inc(V.tensor_tensor(W1p[:, ci * H + g * 256: ci * H + (g + 1) * 256],
                                                       bld_ps[ci % 2][:, 0:256], w1eT[ci % 2][:, :],
                                                       A.add), "sV")

    def bld_wpscale(k):
        for sl in range(4):
            b, e = bld_load(wproj_h[k * 128:(k + 1) * 128, sl * C:(sl + 1) * C])
            wge(S, f"dB{b}", e)
            ev_wpsc[(k, sl)] = inc(S.activation(dumpS[:, :], bld_bufs[b][:, :], AF.Abs,
                                                accum_out=bcol[:, 4 + sl:5 + sl]), "sS")
            bld_done("sS", ev_wpsc[(k, sl)])
        wge(V, "sS", ev_wpsc[(k, 3)])
        vsy(V.tensor_tensor(bcol[:, 4:5], bcol[:, 4:5], bcol[:, 5:6], A.add))
        vsy(V.tensor_tensor(bcol[:, 6:7], bcol[:, 6:7], bcol[:, 7:8], A.add))
        vsy(V.tensor_tensor(bcol[:, 4:5], bcol[:, 4:5], bcol[:, 6:7], A.add))
        vsy(V.tensor_scalar(wpsc[:, 3 * k:3 * k + 1], bcol[:, 4:5], 1.0 / H, 1e-5, A.mult, A.max))
        vsy(V.tensor_scalar(wpsc[:, 3 * k + 1:3 * k + 2], wpsc[:, 3 * k:3 * k + 1], 0.5, None, A.mult))
        ev = vsy(V.tensor_scalar(wpsc[:, 3 * k + 2:3 * k + 3], wpsc[:, 3 * k:3 * k + 1],
                                 -0.5, None, A.mult))
        ev_wpsc[(k, "done")] = ev

    def bld_wptern(si):
        k, sl = si // 4, si % 4
        b, e = bld_load(wproj_h[k * 128:(k + 1) * 128, sl * C:(sl + 1) * C])
        wge(V, f"dB{b}", e)
        wge(V, "sV", ev_wpsc[(k, "done")])
        if si >= 1:
            wge(V, "dC", ev_w2te.get((si - 1, 1), 0))
        for hf in range(4):
            sl_ = slice(hf * 256, (hf + 1) * 256)
            V.tensor_scalar(hi_s[:, :], bld_bufs[b][:, sl_], wpsc[:, 3 * k + 1:3 * k + 2],
                            wpsc[:, 3 * k:3 * k + 1], A.is_gt, A.mult)
            V.tensor_scalar(lo_s[:, :], bld_bufs[b][:, sl_], wpsc[:, 3 * k + 2:3 * k + 3],
                            wpsc[:, 3 * k:3 * k + 1], A.is_lt, A.mult)
            e = V.tensor_tensor(W2e_sl[:, hf * 256:(hf + 1) * 256],
                                hi_s[:, :], lo_s[:, :], A.subtract)
        ev_w2sl[si] = inc(e, "sV")
        bld_done("sV", ev_w2sl[si])
        wge(SY, "sV", ev_w2sl[si])
        ew = inc(SY.dma_start(w2scr[k * 128:(k + 1) * 128, sl * 1024:(sl + 1) * 1024], W2e_sl[:, :]),
                 "dZ2", 16)
        ev_w2w[si] = ew
        wge(SY, "dZ2", ew)
        for m in range(8):
            hm = sl * 8 + m
            e = inc(SY.dma_start(W2eT[:, hm * C + k * 128: hm * C + (k + 1) * 128],
                                 w2scr[k * 128:(k + 1) * 128, hm * 128:(hm + 1) * 128],
                                 transpose=True), "dC", 16)
        ev_w2te[(si, 0)] = ev_w2te[(si, 1)] = e

    def bld_units(t):
        if t < HCH:
            bld_wfc(t)
        if t < CH:
            bld_wpscale(t)
        if 8 <= t < 24:
            bld_wptern((t - 8) * 2)
            bld_wptern((t - 8) * 2 + 1)

    for s_ in ("dG0", "dG1", "dG2"):
        wge(SY, s_, cnt[s_])
        wge(S, s_, cnt[s_])
    wge(SY, "sS", ev_mid_done_s)
    wge(SY, "sV", ev_mid_done_v)
    wge(SY, "sT", cnt["sT"])
    wge(TE, "sS", ev_mid_done_s)
    wge(TE, "sV", ev_mid_done_v)
    barrier_mode[0] = False
    for _k in range(CH):
        bld_wpscale(_k)
    for _si in range(4 * CH):
        bld_wptern(_si)
    for _kk in range(HCH):
        bld_wfc(_kk)
    barrier_mode[0] = False
    ev_bld_s = cnt["sS"]
    ev_bld_v = cnt["sV"]
    wge(TE, "sV", ev_bld_v)
    wge(TE, "sS", ev_bld_s)
    wge(SY, "sV", ev_bld_v)
    wge(GP, "sV", ev_bld_v)
    wge(GP, "sS", ev_bld_s)
    wge(GP, "sT", cnt["sT"])
    p2a_load(0)
    for t in range(NT):
        p2a_tp(t)
        for j in range(2):
            u = 2 * t + j
            wge(TE, f"dT{t % 2}", ev2_tpe[2 * t + 1])
            wge(TE, f"dU{t % 2}", ev2_tpe2[2 * t + 1])
            wge(TE, "sS", ev_mid_done_s)
            if t >= 1:
                wge(TE, "sS", ev2_zwe[2 * (t - 1) + j])
            for ci in range(CH):
                for n in range(2):
                    e = TE.matmul(z_ps[j][:, n * 512:(n + 1) * 512],
                                  atp[t % 2][:, ci * 256 + j * 128: ci * 256 + (j + 1) * 128],
                                  Wvp[:, ci * C + n * 512: ci * C + (n + 1) * 512],
                                  start=(ci == 0), stop=(ci == CH - 1))
            ev2_z[(t, j)] = inc(e, "sT")
            zv = zw[0][:, :]
            pc = p2c[u % 2]
            wge(S, "sT", ev2_z[(t, j)])
            wge(S, "sV", ev_mid_done_v)
            if u >= 1:
                wge(S, "sV", ev2_x1.get(u - 1, 0))
            ev2_zwe[u] = inc(S.activation(zv, z_ps[j][:, :], AF.Copy,
                                          scale=(1.0 if phases == 34 else weights_a[:, u:u + 1])), "sS")
            if phases in (34,) and u < 2:
                wge(SY, "sS", ev2_zwe[u])
                inc(SY.dma_start(y_h[u * 128:(u + 1) * 128, :], zv), "dX", 16)
            wge(V, "sS", ev2_zwe[u])
            ev_pbd = vsy(V.scalar_tensor_tensor(zv, zv, 1.0, pb_rep[:, :], A.mult, A.add,
                                                accum_out=pc[:, 0:1]))
            if phases == 33 and 2 <= u < 4:
                wge(SY, "sV", ev_pbd)
                inc(SY.dma_start(y_h[u * 128:(u + 1) * 128, :], zv), "dX", 16)
            ev = vsy(V.tensor_scalar(pc[:, 2:3], pc[:, 0:1], 1.0 / C, None, A.mult))
            wge(S, "sV", ev)
            e2 = inc(S.activation(dumpS[:, :], zv, AF.Square, accum_out=pc[:, 1:2]), "sS")
            wge(V, "sS", e2)
            vsy(V.tensor_scalar(pc[:, 3:4], pc[:, 1:2], 1.0 / C, EPS, A.mult, A.add))
            vsy(V.tensor_tensor(pc[:, 4:5], pc[:, 2:3], pc[:, 2:3], A.mult))
            e1 = vsy(V.tensor_tensor(pc[:, 5:6], pc[:, 3:4], pc[:, 4:5], A.subtract))
            wge(S, "sV", e1)
            e2 = inc(S.activation(pc[:, 6:7], pc[:, 5:6], AF.Sqrt), "sS")
            wge(V, "sS", e2)
            vsy(V.reciprocal(pc[:, 7:8], pc[:, 6:7]))
            e3 = vsy(V.scalar_tensor_tensor(pc[:, 8:9], pc[:, 2:3], -1.0, pc[:, 7:8], A.mult, A.mult))
            wge(S, "sV", e3)
            V.tensor_scalar(zv, zv, pc[:, 7:8], pc[:, 8:9], A.mult, A.add)
            wge(V, f"dA{u % 3}", ev2_x0[u])
            ev2_x1[u] = inc(V.tensor_tensor(x0p[u % 3][:, :], zv, x0p[u % 3][:, :], A.add), "sV")
            wge(GP, "sV", ev2_x1[u])
            ev2_x1w[u] = inc(GP.dma_start(x1_hb[u * 128:(u + 1) * 128, :], x0p[u % 3][:, :]),
                             f"dG{u % 3}", 16)

        p2a_load(t + 1)

    ev_2a_s = cnt["sS"]
    ev_2a_v = cnt["sV"]
    ev_2a_t = cnt["sT"]
    st_2ap.close()
    if phases == 32:
        for u in range(min(NS, 4)):
            wall(SY)
            ev = inc(SY.dma_start(x0p[0][:, :], x1_hb[u * 128:(u + 1) * 128, :]), "dX", 16)
            wge(SY, "dX", ev)
            inc(SY.dma_start(y_h[u * 128:(u + 1) * 128, :], x0p[0][:, :]), "dX", 16)
        SY.wait_ge(sem["dX"], cnt["dX"])
        st_2as.close()
        st2.close()
        st0.close()
        return nc
    if phases in (33, 34):
        wge(SY, "sS", cnt["sS"])
        wge(SY, "sV", cnt["sV"])
        inc(SY.dma_start(y_h[256:384, 0:NS], weights_a[:, :]), "dX", 16)
        ew = inc(S.activation(zw[0][:, :], Wvp[:, 0:1024], AF.Copy), "sS")
        wge(SY, "sS", ew)
        inc(SY.dma_start(y_h[384:512, :], zw[0][:, :]), "dX", 16)
    st_2as.close()
    if phases <= 4 or phases in (33, 34):
        wall(SY)
        SY.wait_ge(sem["dX"], cnt["dX"])
        st2.close()
        st0.close()
        return nc

    x1p = [P2(nc.sbuf_tensor(f"x1p_{i}", [128, 2048], F32)) for i in range(2)]
    f_sb = [P2(nc.sbuf_tensor(f"f_{i}", [128, 2048], BF16)) for i in range(2)]
    ftp = [P2(nc.sbuf_tensor(f"ftp_{i}", [128, 8 * 256], BF16, is_transpose_dest=True)) for i in range(2)]
    r_s = [P2(nc.sbuf_tensor(f"r_s{i}", [128, 512], F32)) for i in range(2)]
    r2T = [P2(nc.sbuf_tensor(f"r2T_{i}", [128, 512], BF16)) for i in range(2)]
    p3c = [P2(nc.sbuf_tensor(f"p3c_{i}", [128, 12], F32)) for i in range(2)]
    dumpV2 = P2(nc.sbuf_tensor("dumpV2", [128, C], BF16))
    o2e = P2(nc.sbuf_tensor("o2e", [128, 1024], F32))
    st_2bp = contextlib.ExitStack()
    h_ps = [st_2bp.enter_context(nc.psum_tensor(f"h_ps{j}", [128, 512], F32)) for j in range(2)]
    o2_ps = [st_2bp.enter_context(nc.psum_tensor(f"o2_ps{j}", [128, C], F32)) for j in range(2)]

    ev3_x1, ev3_fw, ev3_ft, ev3_fap = {}, {}, {}, {}
    ev3_tpe2, ev3_ft2 = {}, {}
    ev3_tp, ev3_tpe = {}, {}
    ev3_h, ev3_r, ev3_r2, ev3_o2, ev3_x2, ev3_x2w = {}, {}, {}, {}, {}, {}

    def p2b_load(t):
        if t >= NT:
            return
        if t >= 2:
            wge(GP, f"dY{t % 2}", ev3_x2w[t - 2])
            wge(GP, "sV", ev3_x2[(t - 2, 1)])
        xv = x1_hb[t * 256:(t + 1) * 256, :]
        inc(GP.dma_start(x1p[t % 2][:, 0:1024], xv[0:128, :]), f"dA{t % 2}", 16)
        ev3_x1[t] = inc(GP.dma_start(x1p[t % 2][:, 1024:2048], xv[128:256, :]), f"dA{t % 2}", 16)

    def p2b_ln(t):
        if t >= NT:
            return
        for j in range(2):
            u = 2 * t + j
            xv = x1p[t % 2][:, j * 1024:(j + 1) * 1024]
            pc = p3c[u % 2]
            wge(V, f"dA{t % 2}", ev3_x1[t])
            vsy(V.reduce_sum(pc[:, 0:1], xv, axis=AX))
            vsy(V.scalar_tensor_tensor(dumpV2[:, :], xv, 1.0, xv, A.mult, A.mult,
                                       accum_out=pc[:, 1:2]))
            vsy(V.tensor_scalar(pc[:, 2:3], pc[:, 0:1], 1.0 / C, None, A.mult))
            vsy(V.tensor_scalar(pc[:, 3:4], pc[:, 1:2], 1.0 / C, EPS, A.mult, A.add))
            vsy(V.tensor_tensor(pc[:, 4:5], pc[:, 2:3], pc[:, 2:3], A.mult))
            e1 = vsy(V.tensor_tensor(pc[:, 5:6], pc[:, 3:4], pc[:, 4:5], A.subtract))
            wge(S, "sV", e1)
            e2 = inc(S.activation(pc[:, 6:7], pc[:, 5:6], AF.Sqrt), "sS")
            wge(V, "sS", e2)
            vsy(V.reciprocal(pc[:, 7:8], pc[:, 6:7]))
            e3 = vsy(V.scalar_tensor_tensor(pc[:, 8:9], pc[:, 2:3], -1.0, pc[:, 7:8], A.mult, A.mult))
            if t >= 2:
                wge(V, f"dG{(2 * (t - 2) + 1) % 4}", ev3_fw.get(2 * (t - 2) + 1, 0))
            ev3_fap[u] = inc(V.tensor_scalar(f_sb[t % 2][:, j * 1024:(j + 1) * 1024], xv,
                                             pc[:, 7:8], pc[:, 8:9], A.mult, A.add), "sV")
        if phases == 40:
            ev3_fw[t] = cnt[f"dG{t % 4}"]
            ev3_ft[t] = cnt["sS"]
            return
        for j in range(2):
            u = 2 * t + j
            wge(GP, "sV", ev3_fap[u])
            ev3_fw[u] = inc(GP.dma_start(f_hb[u * 128:(u + 1) * 128, :],
                                         f_sb[t % 2][:, j * 1024:(j + 1) * 1024]), f"dG{u % 4}", 16)
        wge(SY, f"dG{(2 * t) % 4}", ev3_fw[2 * t])
        wge(SY, f"dG{(2 * t + 1) % 4}", ev3_fw[2 * t + 1])
        if t >= 2:
            wge(SY, "sT", ev3_h.get((t - 2, 15), 0))
        for ci in range(CH):
            e = inc(SY.dma_start(ftp[t % 2][:, ci * 256:(ci + 1) * 256],
                                 f_hb[t * 256:(t + 1) * 256, ci * 128:(ci + 1) * 128],
                                 transpose=True), f"dT{t % 2}", 16)
        ev3_ft[t] = e
        ev3_ft2[t] = cnt["dU0"] if t % 2 == 0 else cnt["dU1"]

    wge(TE, "sS", ev_2a_s)
    wge(TE, "sV", ev_2a_v)
    wge(TE, "dC", cnt["dC"])
    wge(SY, "sS", ev_2a_s)
    wge(SY, "sV", ev_2a_v)
    wge(SY, "sT", cnt["sT"])
    wge(S, "sT", cnt["sT"])
    wge(GP, "sS", ev_2a_s)
    wge(GP, "sV", ev_2a_v)
    wge(GP, "sT", cnt["sT"])
    for s_ in ("dG0", "dG1", "dG2"):
        wge(GP, s_, cnt[s_])
    p2b_load(0)
    p2b_ln(0)
    p2b_load(1)
    for t in range(NT):
        p2b_ln(t + 1)
        if phases in (40, 41):
            for m in range(2):
                u = 2 * t + m
                wge(V, f"dA{t % 2}", ev3_x1[t])
                wge(V, "sV", ev3_fap[u])
                ev3_x2[(t, m)] = inc(V.tensor_scalar(x1p[t % 2][:, m * 1024:(m + 1) * 1024],
                                                     x1p[t % 2][:, m * 1024:(m + 1) * 1024],
                                                     1.0, None, A.mult), "sV")
            wge(SY, "sV", ev3_x2[(t, 1)])
            yv = y_h[t * 256:(t + 1) * 256, :]
            inc(SY.dma_start(yv[0:128, :], x1p[t % 2][:, 0:1024]), f"dY{t % 2}", 16)
            ev3_x2w[t] = inc(SY.dma_start(yv[128:256, :], x1p[t % 2][:, 1024:2048]), f"dY{t % 2}", 16)
            p2b_load(t + 2)
            continue
        nchunk = 0
        for g in range(17):
            if g < 16:
                wge(TE, f"dT{t % 2}", ev3_ft[t])
                wge(TE, f"dU{t % 2}", ev3_ft2[t])
                wge(TE, "sS", ev_2a_s)
                if g >= 2 or t >= 1:
                    wge(TE, "sV", ev3_r2.get((t, g - 2), ev3_r2.get((t - 1, 14 + g), 0)))
                for hh in range(2):
                    kk = 2 * g + hh
                    for ci in range(CH):
                        e = TE.matmul(h_ps[g % 2][:, hh * 256:(hh + 1) * 256],
                                      W1p[:, ci * H + kk * 128: ci * H + (kk + 1) * 128],
                                      ftp[t % 2][:, ci * 256:(ci + 1) * 256],
                                      start=(hh == 0 and ci == 0), stop=(hh == 1 and ci == CH - 1))
                ev3_h[(t, g)] = inc(e, "sT")
                wge(S, "sT", ev3_h[(t, g)])
                if g >= 2:
                    wge(S, "sV", ev3_r2[(t, g - 2)])
                ev3_r[(t, g)] = inc(S.activation(r_s[g % 2][:, :], h_ps[g % 2][:, :], AF.Relu), "sS")
                wge(V, "sS", ev3_r[(t, g)])
                ev3_r2[(t, g)] = inc(V.tensor_tensor(r2T[g % 2][:, :], r_s[g % 2][:, :],
                                                     r_s[g % 2][:, :], A.mult), "sV")
            if g >= 1 and phases != 42:
                gg = g - 1
                wge(TE, "sV", ev3_r2[(t, gg)])
                if gg == 0 and t >= 1:
                    wge(TE, "sV", ev3_x2[(t - 1, 1)])
                for hh in range(2):
                    kk = 2 * gg + hh
                    for m in range(2):
                        for n in range(2):
                            e = TE.matmul(o2_ps[m][:, n * 512:(n + 1) * 512],
                                          r2T[gg % 2][:, hh * 256 + m * 128: hh * 256 + (m + 1) * 128],
                                          W2eT[:, kk * C + n * 512: kk * C + (n + 1) * 512],
                                          start=(kk == 0), stop=(kk == HCH - 1))
                    nchunk += 1
        if phases == 42:
            ev3_o2[t] = inc(TE.matmul(o2_ps[0][:, 0:256], W1p[:, 0:128], ftp[t % 2][:, 0:256],
                                      start=True, stop=True), "sT")
            for m in range(2):
                u = 2 * t + m
                wge(V, "sV", ev3_r2[(t, 15)])
                wge(V, f"dA{t % 2}", ev3_x1[t])
                ev3_x2[(t, m)] = inc(V.tensor_scalar(x1p[t % 2][:, m * 1024:(m + 1) * 1024],
                                                     x1p[t % 2][:, m * 1024:(m + 1) * 1024],
                                                     1.0, None, A.mult), "sV")
            wge(SY, "sV", ev3_x2[(t, 1)])
            yv = y_h[t * 256:(t + 1) * 256, :]
            inc(SY.dma_start(yv[0:128, :], x1p[t % 2][:, 0:1024]), f"dY{t % 2}", 16)
            ev3_x2w[t] = inc(SY.dma_start(yv[128:256, :], x1p[t % 2][:, 1024:2048]), f"dY{t % 2}", 16)
            p2b_load(t + 2)
            continue
        ev3_o2[t] = inc(e, "sT")
        if phases == 44:
            for m in range(2):
                wge(V, "sT", ev3_o2[t])
                wge(V, f"dA{t % 2}", ev3_x1[t])
                ev3_x2[(t, m)] = inc(V.tensor_scalar(x1p[t % 2][:, m * 1024:(m + 1) * 1024],
                                                     x1p[t % 2][:, m * 1024:(m + 1) * 1024],
                                                     1.0, None, A.mult), "sV")
            wge(SY, "sV", ev3_x2[(t, 1)])
            yv = y_h[t * 256:(t + 1) * 256, :]
            inc(SY.dma_start(yv[0:128, :], x1p[t % 2][:, 0:1024]), f"dY{t % 2}", 16)
            ev3_x2w[t] = inc(SY.dma_start(yv[128:256, :], x1p[t % 2][:, 1024:2048]), f"dY{t % 2}", 16)
            p2b_load(t + 2)
            continue
        for m in range(2):
            u = 2 * t + m
            pc = p3c[u % 2]
            o2v = o2e[:, :]
            wge(S, "sT", ev3_o2[t])
            if m == 1:
                wge(S, "sV", ev3_x2[(t, 0)])
            elif t >= 1:
                wge(S, "sV", ev3_x2[(t - 1, 1)])
            eo = inc(S.activation(o2v, o2_ps[m][:, :], AF.Copy, accum_out=pc[:, 0:1]), "sS")
            e2 = inc(S.activation(dumpS[:, :], o2_ps[m][:, :], AF.Square, accum_out=pc[:, 1:2]), "sS")
            wge(V, "sS", e2)
            vsy(V.tensor_scalar(pc[:, 2:3], pc[:, 0:1], 1.0 / C, None, A.mult))
            vsy(V.tensor_scalar(pc[:, 3:4], pc[:, 1:2], 1.0 / C, EPS, A.mult, A.add))
            vsy(V.tensor_tensor(pc[:, 4:5], pc[:, 2:3], pc[:, 2:3], A.mult))
            e1 = vsy(V.tensor_tensor(pc[:, 5:6], pc[:, 3:4], pc[:, 4:5], A.subtract))
            wge(S, "sV", e1)
            e2 = inc(S.activation(pc[:, 6:7], pc[:, 5:6], AF.Sqrt), "sS")
            wge(V, "sS", e2)
            vsy(V.reciprocal(pc[:, 7:8], pc[:, 6:7]))
            vsy(V.scalar_tensor_tensor(pc[:, 8:9], pc[:, 2:3], -1.0, pc[:, 7:8], A.mult, A.mult))
            xv = x1p[t % 2][:, m * 1024:(m + 1) * 1024]
            V.tensor_scalar(xv, xv, pc[:, 8:9], None, A.add)
            ev3_x2[(t, m)] = inc(V.scalar_tensor_tensor(xv, o2v, pc[:, 7:8], xv,
                                                        A.mult, A.add), "sV")
        wge(GP, "sV", ev3_x2[(t, 1)])
        yv = y_h[t * 256:(t + 1) * 256, :]
        inc(GP.dma_start(yv[0:128, :], x1p[t % 2][:, 0:1024]), f"dY{t % 2}", 16)
        ev3_x2w[t] = inc(GP.dma_start(yv[128:256, :], x1p[t % 2][:, 1024:2048]), f"dY{t % 2}", 16)
        p2b_load(t + 2)

    wall(SY)
    st_2bp.close()
    st2.close()
    st0.close()
    return nc


_NC_CACHE = {}


def kernel(L0_attn_x, L0_attn_qkv_w, L0_attn_c1, L0_attn_c2, L0_attn_conv1_w,
           L0_attn_gn_g, L0_attn_gn_b, L0_attn_conv2_w, L0_attn_conv2_b,
           L0_attn_ch_w1, L0_attn_ch_b1, L0_attn_ch_w2, L0_attn_ch_b2,
           L0_attn_proj_w, L0_attn_proj_b, L0_ffn_fast_b, L0_ffn_w_fc,
           L0_ffn_w_proj, L0_ffn_u, L0_ffn_v, L0_ffn_gate):
    x = np.asarray(L0_attn_x, dtype=np.float32)
    B, N, Cx = x.shape
    f32 = lambda a: np.ascontiguousarray(np.asarray(a, dtype=np.float32))
    qkv_w = f32(L0_attn_qkv_w)
    smalls = np.zeros((1, S_N), dtype=np.float32)
    smalls[0, S_C1W:S_C1W + 12] = f32(L0_attn_conv1_w).reshape(-1)
    smalls[0, S_C2W:S_C2W + 12] = f32(L0_attn_conv2_w).reshape(-1)
    smalls[0, S_GNG:S_GNG + 4] = f32(L0_attn_gn_g).reshape(-1)
    smalls[0, S_GNB:S_GNB + 4] = f32(L0_attn_gn_b).reshape(-1)
    smalls[0, S_C2B] = np.float32(np.asarray(L0_attn_conv2_b).reshape(-1)[0])
    smalls[0, S_C1] = np.float32(np.asarray(L0_attn_c1))
    smalls[0, S_C2] = np.float32(np.asarray(L0_attn_c2))
    smalls[0, S_GATE] = np.float32(np.asarray(L0_ffn_gate))

    common = {
        "qkv_w": qkv_w,
        "qkv_wT": np.ascontiguousarray(qkv_w.T),
        "ch_w1": f32(L0_attn_ch_w1),
        "ch_b1": f32(L0_attn_ch_b1).reshape(1, 256),
        "ch_w2": f32(L0_attn_ch_w2),
        "ch_b2": f32(L0_attn_ch_b2).reshape(1, C),
        "proj_w": f32(L0_attn_proj_w),
        "proj_b": f32(L0_attn_proj_b).reshape(1, C),
        "w_fc": f32(L0_ffn_w_fc),
        "w_proj": f32(L0_ffn_w_proj),
        "uT": np.ascontiguousarray(f32(L0_ffn_u).T),
        "v": f32(L0_ffn_v),
        "fbT": np.ascontiguousarray(f32(L0_ffn_fast_b).T),
        "smalls": smalls,
    }
    if N not in _NC_CACHE:
        _NC_CACHE[N] = build(N)
    nc = _NC_CACHE[N]
    in_maps = [dict(common, x=np.ascontiguousarray(x[i])) for i in range(B)]
    res = run_bass_kernel_spmd(nc, in_maps, core_ids=list(range(B)))
    return np.stack([np.asarray(r["y"], dtype=np.float32) for r in res.results], axis=0)
